# version 04 (final)
# baseline (speedup 1.0000x reference)
import math
import numpy as np
import concourse.bass as bass
import concourse.mybir as mybir
from concourse.bass_utils import run_bass_kernel_spmd

F32 = mybir.dt.float32
BF16 = mybir.dt.bfloat16
I32 = mybir.dt.int32
AF = mybir.ActivationFunctionType
ALU = mybir.AluOpType
AX = mybir.AxisListType.X

T = 2048
D = 1024
NT = T // 128
ND = D // 128
PAD = 8
CW = PAD + T
NCORES = 8
LN_EPS = 1e-5


class Sem:
    __slots__ = ("h", "n")

    def __init__(self, h):
        self.h = h
        self.n = 0

    def tag(self, inst, k=1):
        inst.then_inc(self.h, k)
        self.n += k
        return self.n

    def wait(self, eng, val=None):
        eng.wait_ge(self.h, self.n if val is None else val)


class Ctx:
    def __init__(self, nc, stack, prefix="", sems=None, sem_stack=None,
                 const_stack=None):
        self.nc = nc
        self.stack = stack
        self.prefix = prefix
        self.sems = {} if sems is None else sems
        self.sem_stack = sem_stack if sem_stack is not None else stack
        self.const_stack = const_stack if const_stack is not None else stack
        self.bufs = {}
        self.aps = {}
        self.bar = None
        self.bar_buf = None
        self.is_last = True

    def sem(self, name):
        if name not in self.sems:
            self.sems[name] = Sem(self.sem_stack.enter_context(
                self.nc.semaphore(name)))
        return self.sems[name]

    def sbuf_scoped(self, name, shape, dtype, side="right"):
        cm = self.nc.sbuf_tensor(self.prefix + name, list(shape), dtype, side=side)
        t = cm.__enter__()
        self.bufs[name] = t
        return t, cm

    def sbuf(self, name, shape, dtype):
        t = self.stack.enter_context(
            self.nc.sbuf_tensor(self.prefix + name, list(shape), dtype))
        self.bufs[name] = t
        return t

    def psum_scoped(self, name, shape, dtype=F32):
        cm = self.nc.psum_tensor(self.prefix + name, list(shape), dtype)
        t = cm.__enter__()
        self.bufs[name] = t
        return t, cm

    def psum(self, name, shape, dtype=F32):
        t = self.stack.enter_context(
            self.nc.psum_tensor(self.prefix + name, list(shape), dtype))
        self.bufs[name] = t
        return t

    @property
    def pv(self):
        return self.sem("p_v")

    @property
    def ps_(self):
        return self.sem("p_s")

    @property
    def pt(self):
        return self.sem("p_t")

    @property
    def pg(self):
        return self.sem("p_g")

    @property
    def v(self):
        return self.nc.vector

    @property
    def s(self):
        return self.nc.scalar

    @property
    def t(self):
        return self.nc.tensor

    @property
    def g(self):
        return self.nc.gpsimd

    @property
    def q(self):
        return self.nc.sync

    def reg_consts(self, vals):
        reg = self.nc.__dict__.setdefault("_cc_const_reg", set())
        inst = None
        for val in vals:
            if float(val) in reg:
                continue
            reg.add(float(val))
            b = self.const_stack.enter_context(
                self.nc.sbuf_tensor(f"cst_{len(reg)}", [128, 1], F32))
            inst = self.nc.gpsimd.memset(b[:], float(val))
            self.nc.const_aps.register(F32, float(val), b[:])
        if inst is not None:
            self.pg.tag(inst)


def xt(c, j):
    return c.bufs["x_tm"][:, j * D:(j + 1) * D]


def cmc(buf, ch, lo=0, hi=T):
    return buf[:, ch * CW + PAD + lo: ch * CW + PAD + hi]



def emit_ln_entry(c, dst_cm, rms_norm=False, stat_pref="e", rms_eps=1e-6):
    nc, v, s, t_, g = c.nc, c.v, c.s, c.t, c.g
    st6 = c.sbuf(stat_pref + "_st6", [128, 6], F32)
    st6b = c.sbuf(stat_pref + "_st6b", [128, 6], F32)
    mvA = c.sbuf(stat_pref + "_mvA", [128, 2 * NT], F32)
    mvB = c.sbuf(stat_pref + "_mvB", [128, 2 * NT], F32)
    ms = c.sbuf(stat_pref + "_ms", [128, NT], F32)
    vv = c.sbuf(stat_pref + "_vv", [128, NT], F32)
    dd = c.sbuf(stat_pref + "_dd", [128, NT], F32)
    sq = c.sbuf(stat_pref + "_sq", [128, NT], F32)
    rr = c.sbuf(stat_pref + "_rr", [128, NT], F32)
    nmr = c.sbuf(stat_pref + "_nmr", [128, NT], F32)
    xbt = c.bufs["xbt_s"]

    for j in range(NT):
        v.bn_stats(out=st6[:], in_=xt(c, j)[:, 0:512])
        v.bn_stats(out=st6b[:], in_=xt(c, j)[:, 512:1024])
        v.bn_aggr(out=mvA[:, 2 * j:2 * j + 2], in_=st6[:])
        inst = v.bn_aggr(out=mvB[:, 2 * j:2 * j + 2], in_=st6b[:])
    c.pv.tag(inst)
    m1, v1 = mvA[:, 0:2 * NT:2], mvA[:, 1:2 * NT:2]
    m2, v2 = mvB[:, 0:2 * NT:2], mvB[:, 1:2 * NT:2]
    c.pv.wait(g)
    g.tensor_sub(out=dd[:], in0=m1, in1=m2)
    g.tensor_add(out=ms[:], in0=m1, in1=m2)
    g.tensor_add(out=vv[:], in0=v1, in1=v2)
    g.tensor_mul(out=dd[:], in0=dd[:], in1=dd[:])
    g.tensor_scalar(out=vv[:], in0=vv[:], scalar1=0.5, scalar2=None, op0=ALU.mult)
    g.tensor_scalar(out=dd[:], in0=dd[:], scalar1=0.25, scalar2=None, op0=ALU.mult)
    inst = g.tensor_add(out=vv[:], in0=vv[:], in1=dd[:])
    c.pg.tag(inst)

    c.pg.wait(s)
    inst = s.activation(out=sq[:], in_=vv[:], func=AF.Sqrt, bias=LN_EPS)
    c.ps_.tag(inst)
    c.ps_.wait(v)
    inst = v.reciprocal(out=rr[:], in_=sq[:])
    c.pv.tag(inst)
    if rms_norm:
        mxn2 = c.sbuf(stat_pref + "_mxn2", [128, NT], F32)
        f1 = c.sbuf(stat_pref + "_f1", [128, NT], F32)
        rr0 = c.sbuf(stat_pref + "_rr0", [128, NT], F32)
        nmr0 = c.sbuf(stat_pref + "_nmr0", [128, NT], F32)
        c.pv.wait(g)
        g.tensor_copy(out=rr0[:], in_=rr[:])
        g.tensor_scalar(out=nmr0[:], in0=ms[:], scalar1=-0.5, scalar2=None,
                        op0=ALU.mult)
        g.tensor_mul(out=nmr0[:], in0=nmr0[:], in1=rr0[:])
        g.tensor_mul(out=f1[:], in0=rr[:], in1=rr[:])
        inst = g.tensor_mul(out=mxn2[:], in0=vv[:], in1=f1[:])
        c.pg.tag(inst)
        c.pg.wait(s)
        inst = s.activation(out=f1[:], in_=mxn2[:], func=AF.Sqrt, bias=rms_eps)
        c.ps_.tag(inst)
        c.ps_.wait(v)
        inst = v.reciprocal(out=f1[:], in_=f1[:])
        c.pv.tag(inst)
        c.pv.wait(g)
        inst = g.tensor_mul(out=rr[:], in0=rr[:], in1=f1[:])
        c.pg.tag(inst)
        c.pg.wait(s)
    c.pv.wait(g)
    g.tensor_scalar(out=nmr[:], in0=ms[:], scalar1=-0.5, scalar2=None, op0=ALU.mult)
    inst = g.tensor_mul(out=nmr[:], in0=nmr[:], in1=rr[:])
    c.pg.tag(inst)

    c.pg.wait(s)
    c.pv.wait(s)
    pst = c.bufs["ps_tr"]
    tr_done = {}
    for j in range(NT):
        par = j % 2
        if j >= 2:
            c.pt.wait(s, tr_done[j - 2])
        inst = s.activation(out=xbt[:, par * D:(par + 1) * D], in_=xt(c, j),
                            func=AF.Identity, scale=rr[:, j:j + 1],
                            bias=nmr[:, j:j + 1])
        av = c.ps_.tag(inst)
        c.ps_.wait(t_, av)
        for grp in range(2):
            c.pv.wait(t_)
            for k in range(4):
                ch = grp * 4 + k
                inst = t_.transpose(pst[:, k * 1024: k * 1024 + 128],
                                    xbt[:, par * D + ch * 128: par * D + (ch + 1) * 128],
                                    c.bufs["ident_bf"][:])
            tr_done[j] = c.pt.tag(inst)
            c.pt.wait(v)
            for k in range(4):
                ch = grp * 4 + k
                inst = v.tensor_copy(out=cmc(dst_cm, ch, j * 128, (j + 1) * 128),
                                     in_=pst[:, k * 1024: k * 1024 + 128])
            c.pv.tag(inst)
    return rr, nmr



class ExitLN:

    def __init__(self, c, pref="g", x_stream=False, grp=4):
        self.GRP = grp
        self.c = c
        self.pref = pref
        self.x_stream = x_stream
        if x_stream:
            self.xsb = c.sbuf(pref + "_xsb", [128, self.GRP * D], F32)
            self.sldx2 = c.sem(pref + "_ldx2")
            self.sstx = c.sem(pref + "_stx")
        self.st6 = c.sbuf(pref + "_xst6", [128, 6], F32)
        self.st6b = c.sbuf(pref + "_xst6b", [128, 6], F32)
        self.mvA = c.sbuf(pref + "_xmvA", [128, 2 * NT], F32)
        self.mvB = c.sbuf(pref + "_xmvB", [128, 2 * NT], F32)
        self.ms = c.sbuf(pref + "_xms", [128, NT], F32)
        self.vv = c.sbuf(pref + "_xvv", [128, NT], F32)
        self.dd = c.sbuf(pref + "_xdd", [128, NT], F32)
        self.sq = c.sbuf(pref + "_xsq", [128, NT], F32)
        self.rr = c.sbuf(pref + "_xrr", [128, NT], F32)
        self.nmr = c.sbuf(pref + "_xnmr", [128, NT], F32)
        self.ystage = c.sbuf(pref + "_xyst", [128, self.GRP * D], F32)
        self.grp_tiles = []

    def feed_resid(self, j, psum_src, rs, rb):
        c, v, s, q = self.c, self.c.v, self.c.s, self.c.q
        assert self.x_stream
        self.resid_mode = True
        slotk = len(self.grp_tiles)
        self.sstx.wait(q)
        ld_b = self.sldx2.n
        inst = q.dma_start(out=self.xsb[:, slotk * D:(slotk + 1) * D],
                           in_=self.x_src[j])
        self.sldx2.tag(inst, 16)
        self.sldx2.wait(s, ld_b + 16)
        slot = self.ystage[:, slotk * D:(slotk + 1) * D]
        inst = s.activation(out=slot, in_=self.xsb[:, slotk * D:(slotk + 1) * D],
                            func=AF.Identity, scale=rs[:, j:j + 1],
                            bias=rb[:, j:j + 1])
        sv_ = c.ps_.tag(inst)
        c.ps_.wait(v, sv_)
        v.tensor_tensor(out=slot, in0=slot, in1=psum_src, op=ALU.add)
        v.bn_stats(out=self.st6[:], in_=slot[:, 0:512])
        v.bn_stats(out=self.st6b[:], in_=slot[:, 512:1024])
        v.bn_aggr(out=self.mvA[:, 2 * j:2 * j + 2], in_=self.st6[:])
        inst = v.bn_aggr(out=self.mvB[:, 2 * j:2 * j + 2], in_=self.st6b[:])
        c.pv.tag(inst)
        self.grp_tiles.append(j)
        if len(self.grp_tiles) == self.GRP:
            self.finalize()

    def feed(self, j, src):
        c, v = self.c, self.c.v
        v.bn_stats(out=self.st6[:], in_=src[:, 0:512])
        v.bn_stats(out=self.st6b[:], in_=src[:, 512:1024])
        v.bn_aggr(out=self.mvA[:, 2 * j:2 * j + 2], in_=self.st6[:])
        v.bn_aggr(out=self.mvB[:, 2 * j:2 * j + 2], in_=self.st6b[:])
        slot = len(self.grp_tiles)
        inst = v.tensor_copy(out=self.ystage[:, slot * D:(slot + 1) * D], in_=src)
        c.pv.tag(inst)
        self.grp_tiles.append(j)
        if len(self.grp_tiles) == self.GRP:
            self.finalize()

    def finalize(self):
        c = self.c
        v, s, g = c.v, c.s, c.g
        tiles = self.grp_tiles
        self.grp_tiles = []
        if not tiles:
            return
        j0, j1 = tiles[0], tiles[-1] + 1
        nj = j1 - j0
        assert tiles == list(range(j0, j1))
        sl = slice(2 * j0, 2 * j1, 2)
        sl1 = slice(2 * j0 + 1, 2 * j1, 2)
        slr = slice(j0, j1)
        m1, v1 = self.mvA[:, sl], self.mvA[:, sl1]
        m2, v2 = self.mvB[:, sl], self.mvB[:, sl1]
        dd, ms, vv = self.dd[:, slr], self.ms[:, slr], self.vv[:, slr]
        c.pv.wait(g)
        g.tensor_sub(out=dd, in0=m1, in1=m2)
        g.tensor_add(out=ms, in0=m1, in1=m2)
        g.tensor_add(out=vv, in0=v1, in1=v2)
        g.tensor_mul(out=dd, in0=dd, in1=dd)
        g.tensor_scalar(out=vv, in0=vv, scalar1=0.5, scalar2=None, op0=ALU.mult)
        g.tensor_scalar(out=dd, in0=dd, scalar1=0.25, scalar2=None, op0=ALU.mult)
        inst = g.tensor_add(out=vv, in0=vv, in1=dd)
        c.pg.tag(inst)
        c.pg.wait(s)
        inst = s.activation(out=self.sq[:, slr], in_=vv, func=AF.Sqrt, bias=LN_EPS)
        c.ps_.tag(inst)
        c.ps_.wait(v)
        inst = v.reciprocal(out=self.rr[:, slr], in_=self.sq[:, slr])
        c.pv.tag(inst)
        c.pv.wait(g)
        g.tensor_scalar(out=self.nmr[:, slr], in0=ms, scalar1=-0.5, scalar2=None, op0=ALU.mult)
        inst = g.tensor_mul(out=self.nmr[:, slr], in0=self.nmr[:, slr], in1=self.rr[:, slr])
        c.pg.tag(inst)
        c.pg.wait(s)
        resid = getattr(self, "resid_mode", False)
        ld_base = self.sldx2.n if self.x_stream else 0
        if self.x_stream and not resid:
            q = c.q
            self.sstx.wait(q)
            for k in range(nj):
                inst = q.dma_start(out=self.xsb[:, k * D:(k + 1) * D],
                                   in_=self.x_src[j0 + k])
                self.sldx2.tag(inst, 16)
        for k in range(nj):
            j = j0 + k
            slot = self.ystage[:, k * D:(k + 1) * D]
            inst = s.activation(out=slot, in_=slot, func=AF.Identity,
                                scale=self.rr[:, j:j + 1], bias=self.nmr[:, j:j + 1])
            vs = c.ps_.tag(inst)
            c.ps_.wait(v, vs)
            if self.x_stream:
                if not resid:
                    self.sldx2.wait(v, ld_base + (k + 1) * 16)
                xsl = self.xsb[:, k * D:(k + 1) * D]
                inst = v.tensor_tensor(out=xsl, in0=xsl, in1=slot, op=ALU.add)
                av = c.pv.tag(inst)
                c.pv.wait(c.q, av)
                inst = c.q.dma_start(out=self.x_dst[j], in_=xsl)
                self.sstx.tag(inst, 16)
            else:
                inst = v.tensor_tensor(out=xt(c, j), in0=xt(c, j), in1=slot, op=ALU.add)
                c.pv.tag(inst)



def emit_l3_ffn(c):
    nc, v, s, t_, g, q = c.nc, c.v, c.s, c.t, c.g, c.q
    H = 3072
    TG = 512
    xn = c.bufs["xn_cm"]
    h_sb = c.bufs["scr_bf"]
    wd = c.sbuf("l3f_wd", [128, 24 * D], BF16)
    wgu = c.sbuf("l3f_wgu", [128, 2 * 2 * D], BF16)
    sw = c.sem("l3f_w")
    swd = c.sem("l3f_wd_s")
    wdT = c.aps["l3_wdT"].rearrange("(a p) e -> a p e", p=128)
    for a in range(24):
        inst = g.dma_start(out=wd[:, a * D:(a + 1) * D], in_=wdT[a])
        swd.tag(inst, 16)

    ps_g = c.psum("l3f_psg", [128, 512], F32)
    ps_u = c.psum("l3f_psu", [128, 512], F32)
    ps_o = c.psum("l3f_pso", [128, D], F32)
    sgact = c.sbuf("l3f_sg", [128, 512], F32)
    if "dbg_g" in getattr(c, "dbg_out", {}):
        c.sbuf("l3f_dbgg", [128, 512], F32)
    ex = c.exit_ln

    wgs = c.aps["l3_wg_s"]
    wus = c.aps["l3_wu_s"]

    def load_wgu(par, hc, wait_val):
        if wait_val is not None:
            c.pt.wait(g, wait_val)
        inst = g.dma_start(out=wgu[:, par * 2 * D: par * 2 * D + D], in_=wgs[hc])
        sw.tag(inst, 16)
        inst = g.dma_start(out=wgu[:, par * 2 * D + D: par * 2 * D + 2 * D], in_=wus[hc])
        sw.tag(inst, 16)

    nsteps = (T // TG) * 24
    load_wgu(0, 0, None)
    nload = 1
    step_end_pt = []
    for tg in range(T // TG):
        for hc in range(24):
            i = tg * 24 + hc
            par = i % 2
            if i == 0 and "dbg_w" in getattr(c, "dbg_out", {}):
                g.wait_ge(sw.h, 32)
                sdw = c.sem("dbgw")
                inst = g.dma_start(out=c.dbg_out["dbg_w"], in_=wgu[:, 0:D])
                sdw.tag(inst, 16)
                sdw.wait(g)
            if nload < nsteps:
                wv = step_end_pt[nload - 2] if nload >= 2 else None
                load_wgu(nload % 2, nload % 24, wv)
                nload += 1
            t_.wait_ge(sw.h, (i + 1) * 32)
            lo = tg * TG
            for a in range(ND):
                inst = t_.matmul(ps_g[:], wgu[:, par * 2 * D + a * 128: par * 2 * D + (a + 1) * 128],
                                 cmc(xn, a, lo, lo + 512),
                                 start=(a == 0), stop=(a == ND - 1))
            vg = c.pt.tag(inst)
            for a in range(ND):
                inst = t_.matmul(ps_u[:], wgu[:, par * 2 * D + D + a * 128: par * 2 * D + D + (a + 1) * 128],
                                 cmc(xn, a, lo, lo + 512),
                                 start=(a == 0), stop=(a == ND - 1))
            vu = c.pt.tag(inst)
            c.pt.wait(s, vg)
            if i == 0 and "dbg_g" in getattr(c, "dbg_out", {}):
                inst = s.copy(out=c.bufs["l3f_dbgg"][:], in_=ps_g[:])
                c.ps_.tag(inst)
            inst = s.activation(out=sgact[:], in_=ps_g[:], func=AF.Silu, scale=1.0 / 127.0)
            c.ps_.tag(inst)
            c.pt.wait(v, vu)
            c.ps_.wait(v)
            inst = v.scalar_tensor_tensor(
                out=h_sb[:, hc * TG: hc * TG + 512],
                in0=ps_u[:], scalar=1.0 / 127.0,
                in1=sgact[:], op0=ALU.mult, op1=ALU.mult)
            c.pv.tag(inst)
            c.pv.wait(t_)
            step_end_pt.append(c.pt.n)
        if tg == 0:
            dbg_dump(c, "dbg_h", h_sb[:, 0:4096])
            if "dbg_g" in getattr(c, "dbg_out", {}):
                dbg_dump(c, "dbg_g", c.bufs["l3f_dbgg"][:])
        if tg == 0:
            swd.wait(t_)
        for jj in range(TG // 128):
            j = tg * (TG // 128) + jj
            for hc in range(24):
                for n2 in range(2):
                    inst = t_.matmul(ps_o[:, n2 * 512:(n2 + 1) * 512],
                                     h_sb[:, hc * TG + jj * 128: hc * TG + (jj + 1) * 128],
                                     wd[:, hc * D + n2 * 512: hc * D + (n2 + 1) * 512],
                                     start=(hc == 0), stop=(hc == 23))
            c.pt.tag(inst)
            c.pt.wait(v)
            ex.feed(j, ps_o[:])
            c.pv.wait(t_)


def emit_l1_ffn(c):
    nc, v, s, t_, g, q = c.nc, c.v, c.s, c.t, c.g, c.q
    H = 4096
    NH = H // 128
    TG = 256
    xn = c.bufs["xn_cm"]
    h_sb = c.bufs["scr_bf"]
    ex = c.exit_ln

    w1q = c.sbuf("l1f_w1q", [128, NH * D], BF16)
    w2q, _ = c.sbuf_scoped("l1f_w2q", [128, NH * D], BF16, side="right")
    wstg = c.sbuf("l1f_wstg", [128, 2 * 512], F32)
    zbuf = c.sbuf("l1f_z", [128, 512], F32)
    gcols = c.sbuf("l1f_gc", [128, 128], F32)
    gtot = c.sbuf("l1f_gt", [128, 1], F32)
    invg = c.sbuf("l1f_ig", [128, 2], F32)
    ones_bf = c.sbuf("l1f_1bf", [1, 128], BF16)
    b1sb = c.sbuf("l1f_b1", [128, NH], F32)
    b2sb = c.sbuf("l1f_b2", [1, D], BF16)
    ps_sc = c.psum("l1f_pssc", [128, 512], F32)
    ps_o = c.psum("l1f_pso", [128, D], F32)
    sw = c.sem("l1f_w")
    sw2 = c.sem("l1f_w2")

    w1s_raw = c.aps["l1_w1_s"]
    w2r_raw = c.aps["l1_w2T"].rearrange("(a p) e -> a p e", p=128)
    w1s = lambda a: w1s_raw[a // 2][:, (a % 2) * 512:(a % 2 + 1) * 512]
    w2r = lambda a: w2r_raw[a // 2][:, (a % 2) * 512:(a % 2 + 1) * 512]

    inst = g.dma_start(out=ones_bf[:], in_=c.aps["ones1"])
    c.sem("l1f_cst").tag(inst, 16)
    inst = g.dma_start(out=b1sb[:], in_=c.aps["l1_b1"])
    c.sem("l1f_cst").tag(inst, 16)
    inst = g.dma_start(out=b2sb[:], in_=c.aps["l1_b2"])
    c.sem("l1f_cst").tag(inst, 16)

    def gamma_pass(srcs, col0):
        for a in range(2 * NH):
            par = a % 2
            if a >= 2:
                c.ps_.wait(q, gp_tags[a - 2])
            inst = q.dma_start(out=wstg[:, par * 512:(par + 1) * 512], in_=srcs(a))
            sw.tag(inst, 16)
            sw.wait(s)
            inst = s.activation(out=zbuf[:], in_=wstg[:, par * 512:(par + 1) * 512],
                                func=AF.Abs, accum_out=gcols[:, col0 + a:col0 + a + 1])
            gp_tags[a] = c.ps_.tag(inst)

    gp_tags = {}
    gamma_pass(w1s, 0)
    c.ps_.wait(q)
    gp_tags = {}
    gamma_pass(w2r, 64)
    psg = ps_sc
    ones_col = c.bufs["onescol_f"]
    ones_row = c.bufs["onesrow_f"]
    sb11 = zbuf[0:1, 0:1]
    for mi in range(2):
        c.ps_.wait(v)
        inst = v.tensor_reduce(out=gtot[:], in_=gcols[:, mi * 64:(mi + 1) * 64],
                               axis=AX, op=ALU.add)
        vv_ = c.pv.tag(inst)
        c.pv.wait(t_, vv_)
        inst = t_.matmul(psg[0:1, 0:1], gtot[:], ones_col[:], start=True, stop=True)
        tv = c.pt.tag(inst)
        c.pt.wait(s, tv)
        inst = s.copy(out=sb11, in_=psg[0:1, 0:1])
        sv_ = c.ps_.tag(inst)
        c.ps_.wait(t_, sv_)
        inst = t_.matmul(psg[:, 0:1], ones_row[:], sb11, start=True, stop=True)
        tv = c.pt.tag(inst)
        c.pt.wait(s, tv)
        inst = s.activation(out=invg[:, mi:mi + 1], in_=psg[:, 0:1],
                            func=AF.Identity, scale=1.0 / (4096.0 * 1024.0),
                            bias=LN_EPS)
        c.ps_.tag(inst)
    c.ps_.wait(v)
    inst = v.reciprocal(out=invg[:], in_=invg[:])
    c.pv.tag(inst)
    c.pv.wait(g)
    inst = g.tensor_copy(out=invg[:], in_=invg[:])
    c.pg.tag(inst)
    c.pg.wait(v)

    def quant_pass(srcs, dst, col_invg):
        qt = {}
        for a in range(2 * NH):
            par = a % 2
            if a >= 2:
                c.pv.wait(q, qt[a - 2])
            inst = q.dma_start(out=wstg[:, par * 512:(par + 1) * 512], in_=srcs(a))
            sw2.tag(inst, 16)
            sw2.wait(v)
            wsl = wstg[:, par * 512:(par + 1) * 512]
            inst = v.tensor_scalar(out=zbuf[:], in0=wsl,
                                   scalar1=invg[:, col_invg:col_invg + 1],
                                   scalar2=None, op0=ALU.mult)
            zv = c.pv.tag(inst)
            c.pv.wait(g, zv)
            inst = g.tensor_scalar(out=wsl, in0=zbuf[:], scalar1=-0.5,
                                   scalar2=None, op0=ALU.is_lt)
            c.pg.tag(inst)
            c.pg.wait(v)
            v.tensor_scalar(out=zbuf[:], in0=zbuf[:], scalar1=0.5,
                            scalar2=None, op0=ALU.is_gt)
            inst = v.tensor_tensor(out=dst[:, a * 512:(a + 1) * 512], in0=zbuf[:],
                                   in1=wsl, op=ALU.subtract)
            qt[a] = c.pv.tag(inst)

    c.ps_.wait(q)
    quant_pass(w1s, w1q, 0)
    c.pv.wait(q)
    quant_pass(w2r, w2q, 1)

    c.pv.wait(t_)
    c.ps_.wait(t_)
    for tg in range(T // TG):
        lo = tg * TG
        for f in range(NH):
            for a in range(ND):
                inst = t_.matmul(ps_sc[:, 0:TG], w1q[:, f * D + a * 128: f * D + (a + 1) * 128],
                                 cmc(xn, a, lo, lo + TG),
                                 start=(a == 0), stop=(a == ND - 1))
            tv = c.pt.tag(inst)
            c.pt.wait(s, tv)
            inst = s.activation(out=h_sb[:, f * TG: (f + 1) * TG], in_=ps_sc[:, 0:TG],
                                func=AF.Relu, bias=b1sb[:, f:f + 1])
            hv = c.ps_.tag(inst)
            c.ps_.wait(t_, hv)
        for jj in range(TG // 128):
            j = tg * (TG // 128) + jj
            for f in range(NH):
                for n2 in range(2):
                    inst = t_.matmul(ps_o[:, n2 * 512:(n2 + 1) * 512],
                                     h_sb[:, f * TG + jj * 128: f * TG + (jj + 1) * 128],
                                     w2q[:, f * D + n2 * 512: f * D + (n2 + 1) * 512],
                                     start=(f == 0), stop=False)
            for n2 in range(2):
                inst = t_.matmul(ps_o[:, n2 * 512:(n2 + 1) * 512],
                                 ones_bf[:], b2sb[:, n2 * 512:(n2 + 1) * 512],
                                 start=False, stop=True)
            c.pt.tag(inst)
            c.pt.wait(v)
            ex.feed(j, ps_o[:])
            c.pv.wait(t_)


def emit_l0_ffn(c):
    nc, v, s, t_, g, q = c.nc, c.v, c.s, c.t, c.g, c.q
    TG = 512
    xn = c.bufs["xn_cm"]
    ex = c.exit_ln

    wrT = c.sbuf("l0f_wrT", [128, 8 * D], BF16)
    wkS = c.sbuf("l0f_wkS", [128, 8 * D], BF16)
    wvT = c.sbuf("l0f_wvT", [128, 8 * D], BF16)
    mur = c.sbuf("l0f_mur", [128, 8], F32)
    muk = c.sbuf("l0f_muk", [128, 8], F32)
    omr = c.sbuf("l0f_omr", [128, 8], F32)
    omk = c.sbuf("l0f_omk", [128, 8], F32)
    xr_b = c.sbuf("l0f_xr", [128, 8 * TG], BF16)
    xk_b = c.sbuf("l0f_xk", [128, 8 * TG], BF16)
    kact = c.sbuf("l0f_ka", [128, 8 * TG], BF16)
    r_tm = c.sbuf("l0f_r", [128, 4 * D], BF16)
    t1 = c.sbuf("l0f_t1", [128, TG], F32)
    t2 = c.sbuf("l0f_t2", [128, TG], F32)
    yf = c.sbuf("l0f_yf", [128, D], F32)
    ps_o = c.psum("l0f_pso", [128, D], F32)
    ps_r = c.psum("l0f_psr", [128, 512], F32)
    ps_k = c.psum("l0f_psk", [128, 512], F32)
    sw = c.sem("l0f_w")

    wr_ = c.aps["l0_wrT"].rearrange("(a p) e -> a p e", p=128)
    wk_ = c.aps["l0_wk_s"]
    wv_ = c.aps["l0_wvT"].rearrange("(a p) e -> a p e", p=128)
    for a in range(8):
        inst = g.dma_start(out=wrT[:, a * D:(a + 1) * D], in_=wr_[a])
        sw.tag(inst, 16)
        inst = g.dma_start(out=wkS[:, a * D:(a + 1) * D], in_=wk_[a])
        sw.tag(inst, 16)
        inst = g.dma_start(out=wvT[:, a * D:(a + 1) * D], in_=wv_[a])
        sw.tag(inst, 16)
    inst = g.dma_start(out=mur[:], in_=c.aps["l0_mur"])
    sw.tag(inst, 16)
    inst = g.dma_start(out=muk[:], in_=c.aps["l0_muk"])
    sw.tag(inst, 16)
    sw.wait(g)
    g.tensor_scalar(out=omr[:], in0=mur[:], scalar1=-1.0, scalar2=1.0,
                    op0=ALU.mult, op1=ALU.add)
    inst = g.tensor_scalar(out=omk[:], in0=muk[:], scalar1=-1.0, scalar2=1.0,
                           op0=ALU.mult, op1=ALU.add)
    c.pg.tag(inst)
    sw.wait(v)
    c.pg.wait(v)
    sw.wait(t_)

    for tg in range(T // TG):
        lo = tg * TG
        for a in range(ND):
            xs = cmc(xn, a, lo, lo + TG)
            xp = cmc(xn, a, lo - 1, lo + TG - 1)
            v.tensor_scalar(out=t1[:], in0=xs, scalar1=mur[:, a:a + 1],
                            scalar2=None, op0=ALU.mult)
            v.scalar_tensor_tensor(out=xr_b[:, a * TG:(a + 1) * TG], in0=xp,
                                   scalar=omr[:, a:a + 1], in1=t1[:],
                                   op0=ALU.mult, op1=ALU.add)
            v.tensor_scalar(out=t2[:], in0=xs, scalar1=muk[:, a:a + 1],
                            scalar2=None, op0=ALU.mult)
            inst = v.scalar_tensor_tensor(out=xk_b[:, a * TG:(a + 1) * TG], in0=xp,
                                          scalar=omk[:, a:a + 1], in1=t2[:],
                                          op0=ALU.mult, op1=ALU.add)
        lerp_v = c.pv.tag(inst)
        c.pv.wait(t_, lerp_v)
        for jj in range(TG // 128):
            for n2 in range(2):
                for a in range(ND):
                    inst = t_.matmul(ps_r[:], xr_b[:, a * TG + jj * 128: a * TG + (jj + 1) * 128],
                                     wrT[:, a * D + n2 * 512: a * D + (n2 + 1) * 512],
                                     start=(a == 0), stop=(a == ND - 1))
                tv = c.pt.tag(inst)
                c.pt.wait(s, tv)
                inst = s.activation(out=r_tm[:, jj * D + n2 * 512: jj * D + (n2 + 1) * 512],
                                    in_=ps_r[:], func=AF.Sigmoid)
                sv_ = c.ps_.tag(inst)
                c.ps_.wait(t_, sv_)
        for e in range(ND):
            for a in range(ND):
                inst = t_.matmul(ps_k[:], wkS[:, e * D + a * 128: e * D + (a + 1) * 128],
                                 xk_b[:, a * TG: a * TG + TG],
                                 start=(a == 0), stop=(a == ND - 1))
            tv = c.pt.tag(inst)
            c.pt.wait(s, tv)
            s.activation(out=t1[:], in_=ps_k[:], func=AF.Relu)
            inst = s.activation(out=kact[:, e * TG:(e + 1) * TG], in_=t1[:], func=AF.Square)
            sv_ = c.ps_.tag(inst)
            c.ps_.wait(t_, sv_)
        for jj in range(TG // 128):
            j = tg * (TG // 128) + jj
            for n2 in range(2):
                for a in range(ND):
                    inst = t_.matmul(ps_o[:, n2 * 512:(n2 + 1) * 512],
                                     kact[:, a * TG + jj * 128: a * TG + (jj + 1) * 128],
                                     wvT[:, a * D + n2 * 512: a * D + (n2 + 1) * 512],
                                     start=(a == 0), stop=(a == ND - 1))
            tv = c.pt.tag(inst)
            c.pt.wait(v, tv)
            inst = v.tensor_tensor(out=yf[:], in0=ps_o[:],
                                   in1=r_tm[:, jj * D:(jj + 1) * D], op=ALU.mult)
            c.pv.tag(inst)
            ex.feed(j, yf[:])
            c.pv.wait(t_)


DELTAS = (-128, 0, 128, 256, 384, 512)


def emit_l3_attn(c):
    nc, v, s, t_, g, q = c.nc, c.v, c.s, c.t, c.g, c.q
    ex = c.exit_ln
    xn = c.bufs["xn_cm"]

    iwT = c.sbuf("l3a_iw", [128, 8 * D], BF16)
    owT = c.sbuf("l3a_ow", [128, 8 * D], BF16)
    pT = c.sbuf("l3a_pTs", [128, 48 * 512], BF16)
    v_tm = c.sbuf("l3a_v", [128, NT * D], BF16)
    att_g = c.sbuf("l3a_att", [128, 8 * 512], BF16)
    ps_a = c.psum("l3a_psa", [128, 512], F32)
    ps_o = c.psum("l3a_pso", [128, D], F32)
    sw = c.sem("l3a_w")

    iw_ = c.aps["l3a_iwT"].rearrange("(a p) e -> a p e", p=128)
    ow_ = c.aps["l3a_owT"].rearrange("(a p) e -> a p e", p=128)
    pt_ = c.aps["l3a_pT"]
    for a in range(8):
        inst = g.dma_start(out=iwT[:, a * D:(a + 1) * D], in_=iw_[a])
        sw.tag(inst, 16)
        inst = g.dma_start(out=owT[:, a * D:(a + 1) * D], in_=ow_[a])
        sw.tag(inst, 16)
    for h in range(8):
        inst = g.dma_start(out=pT[:, (h * 6) * 512:(h * 6 + 6) * 512],
                           in_=pt_[h])
        sw.tag(inst, 16)
    sw.wait(t_)

    for j in range(NT):
        for n2 in range(2):
            for a in range(ND):
                inst = t_.matmul(ps_o[:, n2 * 512:(n2 + 1) * 512],
                                 cmc(xn, a, j * 128, (j + 1) * 128),
                                 iwT[:, a * D + n2 * 512: a * D + (n2 + 1) * 512],
                                 start=(a == 0), stop=(a == ND - 1))
        tv = c.pt.tag(inst)
        c.pt.wait(v, tv)
        inst = v.tensor_copy(out=v_tm[:, j * D:(j + 1) * D], in_=ps_o[:])
        vv_ = c.pv.tag(inst)
        c.pv.wait(t_, vv_)

    for qg in range(4):
        ds = [d for d in DELTAS if 0 <= qg * 512 + d and qg * 512 + d + 128 <= T]
        for h in range(8):
            first = True
            for d in ds:
                kc = (qg * 512 + d) // 128
                di = DELTAS.index(d)
                inst = t_.matmul(ps_a[:], v_tm[:, (kc % NT) * D + h * 128:
                                               (kc % NT) * D + (h + 1) * 128],
                                 pT[:, (h * 6 + di) * 512:(h * 6 + di + 1) * 512],
                                 start=first, stop=(d == ds[-1]))
                first = False
            tv = c.pt.tag(inst)
            c.pt.wait(v, tv)
            inst = v.tensor_copy(out=att_g[:, h * 512:(h + 1) * 512], in_=ps_a[:])
            vv_ = c.pv.tag(inst)
            c.pv.wait(t_, vv_)
        for jj in range(4):
            j = qg * 4 + jj
            for n2 in range(2):
                for a in range(ND):
                    inst = t_.matmul(ps_o[:, n2 * 512:(n2 + 1) * 512],
                                     att_g[:, a * 512 + jj * 128: a * 512 + (jj + 1) * 128],
                                     owT[:, a * D + n2 * 512: a * D + (n2 + 1) * 512],
                                     start=(a == 0), stop=(a == ND - 1))
            tv = c.pt.tag(inst)
            c.pt.wait(v, tv)
            ex.feed(j, ps_o[:])
            c.pv.wait(t_)


def emit_l0_attn(c):
    nc, v, s, t_, g, q = c.nc, c.v, c.s, c.t, c.g, c.q
    ex = c.exit_ln
    xn = c.bufs["xn_cm"]
    u_bf = c.bufs["scr_bf"]
    sg_bf = c.sbuf("l0a_sg", [128, 8 * T], BF16)
    nw = c.sbuf("l0a_nw", [128, 8], F32)
    inb = c.sbuf("l0a_inb", [128, 16], F32)
    pb = c.sbuf("l0a_pb", [128, 24], F32)
    cw = c.sbuf("l0a_cw", [128, 32], F32)
    cb = c.sbuf("l0a_cb", [128, 8], F32)
    skp = c.sbuf("l0a_skp", [128, 8], F32)
    outb = c.sbuf("l0a_outb", [1, D], BF16)
    ones_bf = c.sbuf("l0a_1bf", [1, 128], BF16)
    wbuf = c.sbuf("l0a_wb", [128, 2 * D], BF16)
    owT = c.sbuf("l0a_ow", [128, 8 * D], BF16)
    upre = c.sbuf("l0a_upre", [128, PAD + T], F32)
    abuf = c.sbuf("l0a_a", [128, T], F32)
    bbuf = c.sbuf("l0a_b", [128, T], F32)
    cbuf = c.sbuf("l0a_c", [128, T], F32)
    sbuf_s = c.sbuf("l0a_s", [128, T], F32)
    ps_p = c.psum("l0a_psp", [128, T], F32)
    ps_o = c.psum("l0a_pso", [128, D], F32)
    sw = c.sem("l0a_w")
    swl = c.sem("l0a_wl")

    inw_s = c.aps["l0_inw_s"]
    pw_s = c.aps["l0_pw_s"]
    ow_ = c.aps["l0_owT"].rearrange("(a p) e -> a p e", p=128)
    for a in range(8):
        inst = g.dma_start(out=owT[:, a * D:(a + 1) * D], in_=ow_[a])
        sw.tag(inst, 16)
    for nm, ap in [("l0_nw", nw), ("l0_inb", inb), ("l0_pb", pb),
                   ("l0_cw", cw), ("l0_cb", cb), ("l0_skip", skp),
                   ("l0_outb", outb), ("ones1", ones_bf)]:
        inst = g.dma_start(out=ap[:], in_=c.aps[nm])
        sw.tag(inst, 16)
    sw.wait(v)
    sw.wait(s)
    sw.wait(t_)

    inst = v.memset(upre[:, 0:PAD], 0.0)
    c.pv.tag(inst)
    for a in range(ND):
        inst = v.tensor_scalar(out=cmc(xn, a), in0=cmc(xn, a),
                               scalar1=nw[:, a:a + 1], scalar2=None, op0=ALU.mult)
    xnw_v = c.pv.tag(inst)
    c.pv.wait(t_, xnw_v)

    def load_slab(par, srcs, i):
        inst = g.dma_start(out=wbuf[:, par * D:(par + 1) * D], in_=srcs[i])
        swl.tag(inst, 16)

    lt = {}
    load_slab(0, inw_s, 0)
    nload = 1
    for e in range(16):
        par = e % 2
        if nload < 16:
            if nload >= 2:
                c.pt.wait(g, lt[nload - 2])
            load_slab(nload % 2, inw_s, nload)
            nload += 1
        t_.wait_ge(swl.h, (e + 1) * 16)
        for n4 in range(4):
            for a in range(ND):
                inst = t_.matmul(ps_p[:, n4 * 512:(n4 + 1) * 512],
                                 wbuf[:, par * D + a * 128: par * D + (a + 1) * 128],
                                 cmc(xn, a, n4 * 512, (n4 + 1) * 512),
                                 start=(a == 0), stop=(a == ND - 1))
        lt[e] = tv = c.pt.tag(inst)
        c.pt.wait(s, tv)
        if e < 8:
            inst = s.activation(out=upre[:, PAD:PAD + T], in_=ps_p[:],
                                func=AF.Identity, bias=inb[:, e:e + 1])
            sv_ = c.ps_.tag(inst)
            c.ps_.wait(v, sv_)
            c.ps_.wait(t_, sv_)
            v.tensor_scalar(out=abuf[:], in0=upre[:, PAD - 3:PAD + T - 3],
                            scalar1=cw[:, e * 4:e * 4 + 1], scalar2=cb[:, e:e + 1],
                            op0=ALU.mult, op1=ALU.add)
            for k in range(1, 4):
                sh = PAD - 3 + k
                inst = v.scalar_tensor_tensor(out=abuf[:], in0=upre[:, sh:sh + T],
                                              scalar=cw[:, e * 4 + k:e * 4 + k + 1],
                                              in1=abuf[:], op0=ALU.mult, op1=ALU.add)
            vv_ = c.pv.tag(inst)
            c.pv.wait(s, vv_)
            inst = s.activation(out=u_bf[:, e * T:(e + 1) * T], in_=abuf[:], func=AF.Silu)
            c.ps_.tag(inst)
        else:
            inst = s.activation(out=sg_bf[:, (e - 8) * T:(e - 7) * T], in_=ps_p[:],
                                func=AF.Silu, bias=inb[:, e:e + 1])
            sv_ = c.ps_.tag(inst)
            c.ps_.wait(t_, sv_)

    c.ps_.wait(t_)

    lt2 = {}
    order = []
    for j in range(ND):
        order += [j, 8 + j, 16 + j]
    c.pt.wait(g)
    load_slab(0, pw_s, order[0])
    load_slab(1, pw_s, order[1])
    nload = 2
    for j in range(ND):
        for piece in range(3):
            i = j * 3 + piece
            par = i % 2
            if nload < 24:
                c.pt.wait(g, lt2[nload - 2]) if (nload - 2) in lt2 else None
                load_slab(nload % 2, pw_s, order[nload])
                nload += 1
            t_.wait_ge(swl.h, (16 + i + 1) * 16)
            for n4 in range(4):
                for a in range(ND):
                    inst = t_.matmul(ps_p[:, n4 * 512:(n4 + 1) * 512],
                                     wbuf[:, par * D + a * 128: par * D + (a + 1) * 128],
                                     u_bf[:, a * T + n4 * 512: a * T + (n4 + 1) * 512],
                                     start=(a == 0), stop=(a == ND - 1))
            lt2[i] = tv = c.pt.tag(inst)
            c.pt.wait(s, tv)
            c.pv.wait(s)
            ecol = order[i]
            dst, fn = [(abuf, AF.Sigmoid), (bbuf, AF.Tanh), (cbuf, AF.Sigmoid)][piece]
            inst = s.activation(out=dst[:], in_=ps_p[:], func=fn,
                                bias=pb[:, ecol:ecol + 1])
            sv_ = c.ps_.tag(inst)
            c.ps_.wait(t_, sv_)
        c.ps_.wait(v)
        inst = v.tensor_tensor(out=bbuf[:], in0=bbuf[:],
                               in1=u_bf[:, j * T:(j + 1) * T], op=ALU.mult)
        inst = v.tensor_tensor_scan(out=sbuf_s[:], data0=abuf[:], data1=bbuf[:],
                                    initial=0.0, op0=ALU.mult, op1=ALU.add)
        inst = v.tensor_tensor(out=cbuf[:], in0=cbuf[:], in1=sbuf_s[:], op=ALU.mult)
        inst = v.scalar_tensor_tensor(out=cbuf[:], in0=u_bf[:, j * T:(j + 1) * T],
                                      scalar=skp[:, j:j + 1], in1=cbuf[:],
                                      op0=ALU.mult, op1=ALU.add)
        inst = v.tensor_tensor(out=sg_bf[:, j * T:(j + 1) * T], in0=cbuf[:],
                               in1=sg_bf[:, j * T:(j + 1) * T], op=ALU.mult)
        yv = c.pv.tag(inst)
        c.pv.wait(t_, yv)

    rr0 = c.bufs["e_rr0"]
    nmr0 = c.bufs["e_nmr0"]
    for j in range(NT):
        for n2 in range(2):
            for a in range(ND):
                inst = t_.matmul(ps_o[:, n2 * 512:(n2 + 1) * 512],
                                 sg_bf[:, a * T + j * 128: a * T + (j + 1) * 128],
                                 owT[:, a * D + n2 * 512: a * D + (n2 + 1) * 512],
                                 start=(a == 0), stop=False)
            inst = t_.matmul(ps_o[:, n2 * 512:(n2 + 1) * 512], ones_bf[:],
                             outb[:, n2 * 512:(n2 + 1) * 512],
                             start=False, stop=True)
        tv = c.pt.tag(inst)
        c.pt.wait(v, tv)
        c.pt.wait(s, tv)
        ex.feed_resid(j, ps_o[:], rr0, nmr0)
        c.pv.wait(t_)


def emit_l1_attn(c):
    nc, v, s, t_, g, q = c.nc, c.v, c.s, c.t, c.g, c.q
    ex = c.exit_ln
    xn = c.bufs["xn_cm"]
    scr = c.bufs["scr_bf"]
    H = 16
    INV96 = 1.0 / math.sqrt(96.0)

    V0, KR, KC, CKV = 0, 16384, 18432, 34816
    QC, QR, CQ = 34816, 38912, 43008
    YT = CQ
    SCRN = 59392

    wO = c.sbuf("l1a_wO", [128, 8 * D], BF16)
    wKR = c.sbuf("l1a_wKR", [128, 8 * 128], BF16)
    wbuf = c.sbuf("l1a_wb", [128, 2 * D], BF16)
    trim = c.sbuf("l1a_trim", [128, 128], F32)
    pt_sb = c.sbuf("l1a_pt", [128, 128], BF16)
    diag_r = c.sbuf("l1a_dr", [128, 128], BF16)
    nm = c.sbuf("l1a_nm", [128, 1], F32)
    den = c.sbuf("l1a_den", [128, 1], F32)
    lnd = c.sbuf("l1a_lnd", [128, 1], F32)
    rden = c.sbuf("l1a_rd", [128, 1], F32)
    rden2 = c.sbuf("l1a_rd2", [128, 1], F32)
    wUK, wUK_ctx = c.sbuf_scoped("l1a_wUK", [128, 8 * 256], BF16, side="left")
    wUV, wUV_ctx = c.sbuf_scoped("l1a_wUV", [128, 2 * D], BF16, side="left")
    wDKV, wDKV_ctx = c.sbuf_scoped("l1a_wDKV", [128, 2 * D], BF16, side="left")
    ps_s = c.psum("l1a_pss", [128, 1024], F32)
    ps_pt = c.psum("l1a_pspt", [128, 1024], F32)
    ps_y = c.psum("l1a_psy", [128, 128], F32)
    ps_o = c.psum("l1a_pso", [128, D], F32)
    sw = c.sem("l1a_w")
    swl = c.sem("l1a_wl")

    p_bf = scr[:, CQ + 1024: CQ + 1024 + T]

    def Ar(off, lo, hi):
        return scr[:, off + lo: off + hi]

    for nm_, dst, n in [("l1_wO", wO, 8), ("l1_uk_s", wUK, 8), ("l1_wUV", wUV, 2),
                        ("l1_dkv_s", wDKV, 2), ("l1_wKR128", wKR, 8)]:
        ap = c.aps[nm_]
        wdt = dst.shape[1] // n
        for a in range(n):
            inst = g.dma_start(out=dst[:, a * wdt:(a + 1) * wdt], in_=ap[a])
            sw.tag(inst, 16)
    inst = g.dma_start(out=trim[:], in_=c.aps["l1_trimask"])
    sw.tag(inst, 16)
    sw.wait(t_)
    sw.wait(v)
    sw.wait(s)

    import os
    l1dbg = os.environ.get("MKL1", "full")
    for e in range(2):
        for n4 in range(4):
            for a in range(ND):
                inst = t_.matmul(ps_s[:, 0:512], wDKV[:, e * D + a * 128:e * D + (a + 1) * 128],
                                 cmc(xn, a, n4 * 512, (n4 + 1) * 512),
                                 start=(a == 0), stop=(a == ND - 1))
            tv = c.pt.tag(inst)
            c.pt.wait(v, tv)
            inst = v.tensor_copy(out=Ar(CKV, e * T + n4 * 512, e * T + (n4 + 1) * 512),
                                 in_=ps_s[:, 0:512])
            vv_ = c.pv.tag(inst)
            c.pv.wait(t_, vv_)
    for j in range(NT):
        for n2 in range(2):
            for e in range(2):
                inst = t_.matmul(ps_o[:, n2 * 512:(n2 + 1) * 512],
                                 Ar(CKV, e * T + j * 128, e * T + (j + 1) * 128),
                                 wUV[:, e * D + n2 * 512: e * D + (n2 + 1) * 512],
                                 start=(e == 0), stop=(e == 1))
        tv = c.pt.tag(inst)
        c.pt.wait(v, tv)
        inst = v.tensor_copy(out=Ar(V0, j * D, (j + 1) * D), in_=ps_o[:])
        vv_ = c.pv.tag(inst)
        c.pv.wait(t_, vv_)
    for e in range(ND):
        for n4 in range(4):
            for kv in range(2):
                inst = t_.matmul(ps_s[:, 0:512], wUK[:, e * 256 + kv * 128:e * 256 + (kv + 1) * 128],
                                 Ar(CKV, kv * T + n4 * 512, kv * T + (n4 + 1) * 512),
                                 start=(kv == 0), stop=(kv == 1))
            tv = c.pt.tag(inst)
            c.pt.wait(v, tv)
            inst = v.tensor_copy(out=Ar(KC, e * T + n4 * 512, e * T + (n4 + 1) * 512),
                                 in_=ps_s[:, 0:512])
            vv_ = c.pv.tag(inst)
            c.pv.wait(t_, vv_)

    if l1dbg == "A":
        return
    c.pt.wait(g)
    wDKV_ctx.__exit__(None, None, None)
    wUV_ctx.__exit__(None, None, None)
    wUK_ctx.__exit__(None, None, None)
    scl = c.sbuf("l1a_scl", [128, T], BF16)
    s_row = c.sbuf("l1a_srow", [128, T], F32)
    inst = g.dma_start(out=scl[:], in_=c.aps["l1_scale128"])
    sw.tag(inst, 16)
    sw.wait(t_)
    sw.wait(v)
    for n4 in range(4):
        for a in range(ND):
            inst = t_.matmul(ps_s[:, 0:512], wKR[:, a * 128:(a + 1) * 128],
                             cmc(xn, a, n4 * 512, (n4 + 1) * 512),
                             start=(a == 0), stop=(a == ND - 1))
        tv = c.pt.tag(inst)
        c.pt.wait(v, tv)
        inst = v.tensor_tensor(out=Ar(KR, n4 * 512, (n4 + 1) * 512), in0=ps_s[:, 0:512],
                               in1=scl[:, n4 * 512:(n4 + 1) * 512], op=ALU.mult)
        vv_ = c.pv.tag(inst)
        c.pv.wait(t_, vv_)

    if l1dbg == "KR":
        return
    dq_s = c.aps["l1_dq_s"]
    uq_s = c.aps["l1_uq_s"]
    qr_s = c.aps["l1_qr_s"]

    for Q in range((int(l1dbg[1:]) // 4 + 1) if l1dbg.startswith("J") else (1 if (l1dbg == "B" or l1dbg.startswith("D")) else 4)) if l1dbg != "BALL" else range(4):
        q0 = Q * 512
        c.pt.wait(g)
        lts = {}
        nload = [0]

        def loadw(src_ap, i, lo, hi):
            par = nload[0] % 2
            if nload[0] >= 2 and (nload[0] - 2) in lts:
                c.pt.wait(g, lts[nload[0] - 2])
            inst = g.dma_start(out=wbuf[:, par * D: par * D + (hi - lo)],
                               in_=src_ap[i][:, lo:hi])
            swl.tag(inst, 16)
            nload[0] += 1
            return par

        for cc in range(32):
            par = loadw(dq_s, cc, 0, 1024)
            t_.wait_ge(swl.h, swl.n)
            for a in range(ND):
                inst = t_.matmul(ps_s[:, 0:512], wbuf[:, par * D + a * 128: par * D + (a + 1) * 128],
                                 cmc(xn, a, q0, q0 + 512),
                                 start=(a == 0), stop=(a == ND - 1))
            lts[nload[0] - 1] = tv = c.pt.tag(inst)
            c.pt.wait(v, tv)
            inst = v.tensor_copy(out=Ar(CQ, cc * 512, (cc + 1) * 512), in_=ps_s[:, 0:512])
            vv_ = c.pv.tag(inst)
            c.pv.wait(t_, vv_)
        for kind in ("uq", "qr"):
            srcs = uq_s if kind == "uq" else qr_s
            for e in range(8):
                for piece in range(4):
                    par = loadw(srcs, e, piece * 1024, (piece + 1) * 1024)
                    t_.wait_ge(swl.h, swl.n)
                    for cc8 in range(8):
                        cc = piece * 8 + cc8
                        inst = t_.matmul(ps_s[:, 0:512],
                                         wbuf[:, par * D + cc8 * 128: par * D + (cc8 + 1) * 128],
                                         Ar(CQ, cc * 512, (cc + 1) * 512),
                                         start=(cc == 0), stop=(cc == 31))
                    lts[nload[0] - 1] = c.pt.tag(inst)
                tv = lts[nload[0] - 1]
                c.pt.wait(v, tv)
                if kind == "uq":
                    inst = v.tensor_copy(out=Ar(QC, e * 512, (e + 1) * 512), in_=ps_s[:, 0:512])
                else:
                    inst = v.tensor_tensor(out=Ar(QR, e * 512, (e + 1) * 512), in0=ps_s[:, 0:512],
                                           in1=scl[:, q0:q0 + 512], op=ALU.mult)
                vv_ = c.pv.tag(inst)
                c.pv.wait(t_, vv_)

        if l1dbg in ("B", "BALL"):
            continue
        jjs = (range(int(l1dbg[1:]) if l1dbg.startswith("D") else 4)
               if not l1dbg.startswith("J") else
               ([int(l1dbg[1:]) - Q * 4] if 0 <= int(l1dbg[1:]) - Q * 4 < 4 else []))
        for jj in jjs:
            j = Q * 4 + jj
            nfull = j // 4
            rem = j % 4
            ceil_k = (nfull + 1) * 512
            nck = nfull * 4 + rem + 1
            tail0 = nfull * 512 + (rem + 1) * 128
            if tail0 < ceil_k:
                inst = v.memset(s_row[:, tail0: ceil_k], 3e38)
                c.pv.tag(inst)
                c.pv.wait(t_)
            for h in range(H):
                chh = (h * 64) // 128
                b0 = (h * 64) % 128

                def qsl(base):
                    o = base + chh * 512 + jj * 128
                    return scr[b0:b0 + 64, o:o + 128]

                def kCsl(lo, hi):
                    return scr[b0:b0 + 64, KC + chh * T + lo: KC + chh * T + hi]

                def kRsl(lo, hi):
                    return scr[b0:b0 + 64, KR + lo: KR + hi]

                dlo = (nck - 1) * 128
                t_.matmul(ps_s[:, 0:128], qsl(QC), kCsl(dlo, dlo + 128),
                          start=True, stop=False)
                inst = t_.matmul(ps_s[:, 0:128], qsl(QR), kRsl(dlo, dlo + 128),
                                 start=False, stop=True)
                tv = c.pt.tag(inst)
                c.pt.wait(v, tv)
                inst = v.scalar_tensor_tensor(out=s_row[:, dlo:dlo + 128], in0=ps_s[:, 0:128],
                                              scalar=-INV96, in1=trim[:],
                                              op0=ALU.mult, op1=ALU.add)
                c.pv.tag(inst)
                c.pv.wait(t_)
                for rb in range(rem):
                    lo = (nfull * 4 + rb) * 128
                    t_.matmul(ps_s[:, 512:640], qsl(QC), kCsl(lo, lo + 128),
                              start=True, stop=False)
                    inst = t_.matmul(ps_s[:, 512:640], qsl(QR), kRsl(lo, lo + 128),
                                     start=False, stop=True)
                    tv = c.pt.tag(inst)
                    c.pt.wait(v, tv)
                    inst = v.tensor_scalar(out=s_row[:, lo:lo + 128], in0=ps_s[:, 512:640],
                                           scalar1=-INV96, scalar2=None, op0=ALU.mult)
                    c.pv.tag(inst)
                    c.pv.wait(t_)
                for kb in range(nfull):
                    reg = (kb % 2) * 512
                    t_.matmul(ps_s[:, reg:reg + 512], qsl(QC),
                              kCsl(kb * 512, (kb + 1) * 512), start=True, stop=False)
                    inst = t_.matmul(ps_s[:, reg:reg + 512], qsl(QR),
                                     kRsl(kb * 512, (kb + 1) * 512), start=False, stop=True)
                    tv = c.pt.tag(inst)
                    c.pt.wait(v, tv)
                    inst = v.tensor_scalar(out=s_row[:, kb * 512:(kb + 1) * 512],
                                           in0=ps_s[:, reg:reg + 512],
                                           scalar1=-INV96, scalar2=None, op0=ALU.mult)
                    c.pv.tag(inst)
                    c.pv.wait(t_)
                inst = v.tensor_reduce(out=nm[:], in_=s_row[:, 0:ceil_k], axis=AX, op=ALU.min)
                nv_ = c.pv.tag(inst)
                c.pv.wait(s, nv_)
                s.activation(out=p_bf[:, 0:ceil_k], in_=s_row[:, 0:ceil_k],
                             func=AF.Exp, scale=-1.0, bias=nm[:], accum_out=den[:])
                s.activation(out=lnd[:], in_=den[:], func=AF.Ln)
                inst = s.activation(out=rden[:], in_=lnd[:], func=AF.Exp, scale=-1.0)
                sv_ = c.ps_.tag(inst)
                c.ps_.wait(g, sv_)
                inst = g.tensor_copy(out=rden2[:], in_=rden[:])
                gv_ = c.pg.tag(inst)
                c.pg.wait(v, gv_)
                inst = v.tensor_scalar(out=diag_r[:], in0=c.bufs["ident_bf"][:],
                                       scalar1=rden2[:], scalar2=None, op0=ALU.mult)
                dv_ = c.pv.tag(inst)
                c.pv.wait(t_, dv_)
                for ck in range(nck):
                    bank = ck % 2
                    inst = t_.matmul(ps_pt[:, bank * 512: bank * 512 + 128],
                                     p_bf[:, ck * 128:(ck + 1) * 128], diag_r[:],
                                     start=True, stop=True)
                    tv = c.pt.tag(inst)
                    c.pt.wait(v, tv)
                    inst = v.tensor_copy(out=pt_sb[:], in_=ps_pt[:, bank * 512: bank * 512 + 128])
                    vv_ = c.pv.tag(inst)
                    c.pv.wait(t_, vv_)
                    inst = t_.matmul(ps_y[0:64, :],
                                     scr[:, V0 + ck * D + h * 64: V0 + ck * D + (h + 1) * 64],
                                     pt_sb[:], start=(ck == 0), stop=(ck == nck - 1))
                c.pt.tag(inst)
                c.pt.wait(v)
                inst = v.tensor_copy(
                    out=scr[(h % 2) * 64:(h % 2) * 64 + 64,
                            YT + (h // 2) * 128: YT + (h // 2) * 128 + 128],
                    in_=ps_y[0:64, :])
                vv_ = c.pv.tag(inst)
                c.pv.wait(t_, vv_)
            for n2 in range(2):
                for a in range(ND):
                    inst = t_.matmul(ps_o[:, n2 * 512:(n2 + 1) * 512],
                                     scr[:, YT + a * 128: YT + (a + 1) * 128],
                                     wO[:, a * D + n2 * 512: a * D + (n2 + 1) * 512],
                                     start=(a == 0), stop=(a == ND - 1))
            tv = c.pt.tag(inst)
            c.pt.wait(v, tv)
            if l1dbg == "NOEXIT":
                inst = v.tensor_copy(out=s_row[:, 0:1024], in_=ps_o[:])
                c.pv.tag(inst)
            else:
                ex.feed(j, ps_o[:])
            c.pv.wait(t_)


def emit_l2_ffn(c):
    nc, v, s, t_, g, q = c.nc, c.v, c.s, c.t, c.g, c.q
    ex = c.exit_ln
    xn = c.bufs["xn_cm"]
    scr = c.bufs["scr_bf"]
    H = 16
    INV8 = 0.125
    Q0, K0 = 0, 16384

    u_spill = nc.dram_tensor("l2f_uspill", [T, D], F32, kind="Internal").ap()
    usr = u_spill.rearrange("(n p) d -> n p d", p=128)

    lnw = c.sbuf("l2f_lnw", [128, 8 * 4], F32)
    ybf = c.sbuf("l2f_ybf", [128, D], BF16)
    cw = c.sbuf("l2f_cw", [128, 24], F32)
    rt = c.sbuf("l2f_rt", [128, 128], BF16)
    trim = c.sbuf("l2f_trim", [128, 128], F32)
    wbuf = c.sbuf("l2f_wb", [128, 2 * D], BF16)
    pt_sb = c.sbuf("l2f_pt", [128, 128], BF16)
    diag_r = c.sbuf("l2f_dr", [128, 128], BF16)
    nmv = c.sbuf("l2f_nm", [128, 1], F32)
    den = c.sbuf("l2f_den", [128, 1], F32)
    lnd = c.sbuf("l2f_lnd", [128, 1], F32)
    rden = c.sbuf("l2f_rd", [128, 1], F32)
    rden2 = c.sbuf("l2f_rd2", [128, 1], F32)
    sw = c.sem("l2f_w")
    swl = c.sem("l2f_wl")
    sux = c.sem("l2f_ux")

    vbuf, vbuf_ctx = c.sbuf_scoped("l2f_v", [128, NT * D], BF16, side="left")
    h1p, h1p_ctx = c.sbuf_scoped("l2f_h1p", [128, PAD + T], F32, side="left")
    cosr, cosr_ctx = c.sbuf_scoped("l2f_cos", [128, T], BF16, side="left")
    sinr, sinr_ctx = c.sbuf_scoped("l2f_sin", [128, T], BF16, side="left")
    rop1, rop1_ctx = c.sbuf_scoped("l2f_ro1", [128, T], F32, side="left")
    awv, awv_ctx = c.sbuf_scoped("l2f_awv", [128, 8 * D], BF16, side="left")
    ps_s, ps_s_ctx = c.psum_scoped("l2f_pss", [128, 1024], F32)
    ps_pt, ps_pt_ctx = c.psum_scoped("l2f_pspt", [128, 1024], F32)
    ps_y, ps_y_ctx = c.psum_scoped("l2f_psy", [128, 128], F32)
    ps_o, ps_o_ctx = c.psum_scoped("l2f_pso", [128, D], F32)

    aw_s = c.aps["l2_aw_s"]
    awr = c.aps["l2_aw_rows"]
    prj = c.aps["l2_projw_r"]
    for a in range(8):
        inst = g.dma_start(out=awv[:, a * D:(a + 1) * D], in_=awr[a][:, 2048:3072])
        sw.tag(inst, 16)
    for nm_, dst in [("l2_lnw", lnw), ("l2_cw", cw), ("l2_rt", rt),
                     ("l1_trimask", trim), ("l2_cosr", cosr), ("l2_sinr", sinr)]:
        inst = g.dma_start(out=dst[:], in_=c.aps[nm_])
        sw.tag(inst, 16)
    sw.wait(v)
    sw.wait(s)
    sw.wait(t_)
    c.pg.wait(v)

    rr1 = c.bufs["e_rr0"]
    nmr1 = c.bufs["e_nmr0"]
    c.x_src_l2 = ex.x_src

    inst = v.memset(h1p[:, 0:PAD], 0.0)
    c.pv.tag(inst)
    for a in range(ND):
        inst = v.tensor_scalar(out=h1p[:, PAD:PAD + T], in0=cmc(xn, a),
                               scalar1=lnw[:, a * 4:a * 4 + 1],
                               scalar2=lnw[:, a * 4 + 1:a * 4 + 2],
                               op0=ALU.mult, op1=ALU.add)
        inst = v.tensor_scalar(out=rop1[:], in0=h1p[:, PAD - 2:PAD + T - 2],
                               scalar1=cw[:, a * 3:a * 3 + 1], scalar2=None,
                               op0=ALU.mult)
        inst = v.scalar_tensor_tensor(out=rop1[:], in0=h1p[:, PAD - 1:PAD + T - 1],
                                      scalar=cw[:, a * 3 + 1:a * 3 + 2], in1=rop1[:],
                                      op0=ALU.mult, op1=ALU.add)
        inst = v.scalar_tensor_tensor(out=cmc(xn, a), in0=h1p[:, PAD:PAD + T],
                                      scalar=cw[:, a * 3 + 2:a * 3 + 3], in1=rop1[:],
                                      op0=ALU.mult, op1=ALU.add)
    hc_v = c.pv.tag(inst)
    c.pv.wait(t_, hc_v)

    for j in range(NT):
        for n2 in range(2):
            for a in range(ND):
                inst = t_.matmul(ps_o[:, n2 * 512:(n2 + 1) * 512],
                                 cmc(xn, a, j * 128, (j + 1) * 128),
                                 awv[:, a * D + n2 * 512: a * D + (n2 + 1) * 512],
                                 start=(a == 0), stop=(a == ND - 1))
        tv = c.pt.tag(inst)
        c.pt.wait(v, tv)
        inst = v.tensor_copy(out=vbuf[:, j * D:(j + 1) * D], in_=ps_o[:])
        vv_ = c.pv.tag(inst)
        c.pv.wait(t_, vv_)
    c.pt.wait(g)
    awv_ctx.__exit__(None, None, None)

    lts = {}
    nload = [0]

    def loadw(ap_i, lo, hi):
        par = nload[0] % 2
        if nload[0] >= 2 and (nload[0] - 2) in lts:
            c.pt.wait(g, lts[nload[0] - 2])
        inst = g.dma_start(out=wbuf[:, par * D: par * D + (hi - lo)], in_=ap_i[:, lo:hi])
        swl.tag(inst, 16)
        nload[0] += 1
        return par

    for qi, base in ((0, Q0), (1, K0)):
        for e in range(ND):
            par = loadw(aw_s[qi * 8 + e], 0, 1024)
            t_.wait_ge(swl.h, swl.n)
            for n4 in range(4):
                for a in range(ND):
                    inst = t_.matmul(ps_s[:, 0:512],
                                     wbuf[:, par * D + a * 128: par * D + (a + 1) * 128],
                                     cmc(xn, a, n4 * 512, (n4 + 1) * 512),
                                     start=(a == 0), stop=(a == ND - 1))
                lts[nload[0] - 1] = tv = c.pt.tag(inst)
                c.pt.wait(v, tv)
                inst = v.tensor_copy(out=scr[:, base + e * T + n4 * 512: base + e * T + (n4 + 1) * 512],
                                     in_=ps_s[:, 0:512])
                vv_ = c.pv.tag(inst)
                c.pv.wait(t_, vv_)

    for base in (Q0, K0):
        for e in range(ND):
            ch = scr[:, base + e * T: base + (e + 1) * T]
            for n4 in range(4):
                inst = t_.matmul(ps_s[:, 0:512], rt[:],
                                 scr[:, base + e * T + n4 * 512: base + e * T + (n4 + 1) * 512],
                                 start=True, stop=True)
                tv = c.pt.tag(inst)
                c.pt.wait(v, tv)
                sl = slice(n4 * 512, (n4 + 1) * 512)
                v.tensor_tensor(out=rop1[:, sl], in0=ch[:, sl], in1=cosr[:, sl], op=ALU.mult)
                v.tensor_tensor(out=h1p[:, PAD + n4 * 512:PAD + (n4 + 1) * 512],
                                in0=ps_s[:, 0:512], in1=sinr[:, sl], op=ALU.mult)
                inst = v.tensor_tensor(out=ch[:, sl], in0=rop1[:, sl],
                                       in1=h1p[:, PAD + n4 * 512:PAD + (n4 + 1) * 512],
                                       op=ALU.add)
                vv_ = c.pv.tag(inst)
                c.pv.wait(t_, vv_)

    c.pv.wait(g)
    c.ps_.wait(g)
    rop1_ctx.__exit__(None, None, None)
    sinr_ctx.__exit__(None, None, None)
    cosr_ctx.__exit__(None, None, None)
    h1p_ctx.__exit__(None, None, None)
    projw, projw_ctx = c.sbuf_scoped("l2f_projw", [128, 8 * D], BF16, side="left")
    s_row, s_row_ctx = c.sbuf_scoped("l2f_srow", [128, T], F32, side="left")
    pbf, pbf_ctx = c.sbuf_scoped("l2f_pbf", [128, T], BF16, side="left")
    ust, ust_ctx = c.sbuf_scoped("l2f_ust", [128, D], F32, side="left")
    xst, xst_ctx = c.sbuf_scoped("l2f_xst", [128, D], F32, side="left")
    for a in range(8):
        inst = g.dma_start(out=projw[:, a * D:(a + 1) * D], in_=prj[a])
        sw.tag(inst, 16)
    sw.wait(t_)
    sw.wait(s)
    sw.wait(v)
    c.ps_.wait(q)
    c.pv.wait(q)

    for j in range(NT):
        nfull = j // 4
        rem = j % 4
        ceil_k = (nfull + 1) * 512
        nck = nfull * 4 + rem + 1
        tail0 = nfull * 512 + (rem + 1) * 128
        if tail0 < ceil_k:
            inst = v.memset(s_row[:, tail0: ceil_k], 3e38)
            c.pv.tag(inst)
            c.pv.wait(t_)
        for h in range(H):
            chh = (h * 64) // 128
            b0 = (h * 64) % 128
            qsl = scr[b0:b0 + 64, Q0 + chh * T + j * 128: Q0 + chh * T + (j + 1) * 128]

            def ksl(lo, hi):
                return scr[b0:b0 + 64, K0 + chh * T + lo: K0 + chh * T + hi]

            dlo = (nck - 1) * 128
            inst = t_.matmul(ps_s[:, 0:128], qsl, ksl(dlo, dlo + 128), start=True, stop=True)
            tv = c.pt.tag(inst)
            c.pt.wait(v, tv)
            inst = v.scalar_tensor_tensor(out=s_row[:, dlo:dlo + 128], in0=ps_s[:, 0:128],
                                          scalar=-INV8, in1=trim[:], op0=ALU.mult, op1=ALU.add)
            c.pv.tag(inst)
            c.pv.wait(t_)
            for rb in range(rem):
                lo = (nfull * 4 + rb) * 128
                inst = t_.matmul(ps_s[:, 512:640], qsl, ksl(lo, lo + 128), start=True, stop=True)
                tv = c.pt.tag(inst)
                c.pt.wait(v, tv)
                inst = v.tensor_scalar(out=s_row[:, lo:lo + 128], in0=ps_s[:, 512:640],
                                       scalar1=-INV8, scalar2=None, op0=ALU.mult)
                c.pv.tag(inst)
                c.pv.wait(t_)
            for kb in range(nfull):
                reg = (kb % 2) * 512
                inst = t_.matmul(ps_s[:, reg:reg + 512], qsl, ksl(kb * 512, (kb + 1) * 512),
                                 start=True, stop=True)
                tv = c.pt.tag(inst)
                c.pt.wait(v, tv)
                inst = v.tensor_scalar(out=s_row[:, kb * 512:(kb + 1) * 512],
                                       in0=ps_s[:, reg:reg + 512],
                                       scalar1=-INV8, scalar2=None, op0=ALU.mult)
                c.pv.tag(inst)
                c.pv.wait(t_)
            inst = v.tensor_reduce(out=nmv[:], in_=s_row[:, 0:ceil_k], axis=AX, op=ALU.min)
            nv_ = c.pv.tag(inst)
            c.pv.wait(s, nv_)
            s.activation(out=pbf[:, 0:ceil_k], in_=s_row[:, 0:ceil_k],
                         func=AF.Exp, scale=-1.0, bias=nmv[:], accum_out=den[:])
            s.activation(out=lnd[:], in_=den[:], func=AF.Ln)
            inst = s.activation(out=rden[:], in_=lnd[:], func=AF.Exp, scale=-1.0)
            sv_ = c.ps_.tag(inst)
            c.ps_.wait(g, sv_)
            inst = g.tensor_copy(out=rden2[:], in_=rden[:])
            gv_ = c.pg.tag(inst)
            c.pg.wait(v, gv_)
            inst = v.tensor_scalar(out=diag_r[:], in0=c.bufs["ident_bf"][:],
                                   scalar1=rden2[:], scalar2=None, op0=ALU.mult)
            dv_ = c.pv.tag(inst)
            c.pv.wait(t_, dv_)
            for ck in range(nck):
                bank = ck % 2
                inst = t_.matmul(ps_pt[:, bank * 512: bank * 512 + 128],
                                 pbf[:, ck * 128:(ck + 1) * 128], diag_r[:],
                                 start=True, stop=True)
                tv = c.pt.tag(inst)
                c.pt.wait(v, tv)
                inst = v.tensor_copy(out=pt_sb[:], in_=ps_pt[:, bank * 512: bank * 512 + 128])
                vv_ = c.pv.tag(inst)
                c.pv.wait(t_, vv_)
                inst = t_.matmul(ps_y[0:64, :],
                                 vbuf[:, ck * D + h * 64: ck * D + (h + 1) * 64],
                                 pt_sb[:], start=(ck == 0), stop=(ck == nck - 1))
            c.pt.tag(inst)
            c.pt.wait(v)
            inst = v.tensor_copy(
                out=ybf[(h % 2) * 64:(h % 2) * 64 + 64,
                        (h // 2) * 128:(h // 2) * 128 + 128],
                in_=ps_y[0:64, :])
            vv_ = c.pv.tag(inst)
            c.pv.wait(t_, vv_)
        for n2 in range(2):
            for a in range(ND):
                inst = t_.matmul(ps_o[:, n2 * 512:(n2 + 1) * 512],
                                 ybf[:, a * 128:(a + 1) * 128],
                                 projw[:, a * D + n2 * 512: a * D + (n2 + 1) * 512],
                                 start=(a == 0), stop=(a == ND - 1))
        tv = c.pt.tag(inst)
        sux.wait(q)
        inst = q.dma_start(out=xst[:], in_=c.x_src_l2[j])
        c.sem("l2f_lx").tag(inst, 16)
        c.sem("l2f_lx").wait(s)
        inst = s.activation(out=ust[:], in_=xst[:], func=AF.Identity,
                            scale=rr1[:, j:j + 1], bias=nmr1[:, j:j + 1])
        sv_ = c.ps_.tag(inst)
        c.ps_.wait(v, sv_)
        c.pt.wait(v, tv)
        inst = v.tensor_tensor(out=ust[:], in0=ust[:], in1=ps_o[:], op=ALU.add)
        uv_ = c.pv.tag(inst)
        c.pv.wait(q, uv_)
        inst = q.dma_start(out=usr[j], in_=ust[:])
        sux.tag(inst, 16)
        c.pv.wait(t_)
    sux.wait(q)

    c.pv.wait(g)
    c.ps_.wait(g)
    c.pt.wait(g)
    for ctx in (ps_o_ctx, ps_y_ctx, ps_pt_ctx, ps_s_ctx):
        ctx.__exit__(None, None, None)
    for ctx in (xst_ctx, ust_ctx, pbf_ctx, s_row_ctx, projw_ctx, vbuf_ctx):
        ctx.__exit__(None, None, None)
    u_tm = c.sbuf("l2f_u", [128, NT * D], F32)
    logit = c.sbuf("l2f_lg", [128, 64], F32)
    masks = c.sbuf("l2f_mk", [128, 64], F32)
    mtmp = c.sbuf("l2f_mt", [128, 16], F32)
    c.sbuf("l2f_mt2", [128, 16], F32)
    sgl = c.sbuf("l2f_sg", [128, 512], F32)
    ps_tr2, ps_tr2_ctx = c.psum_scoped("l2f_ptr2", [128, 4096], BF16)
    c.bufs["ps_tr"] = ps_tr2
    sul = c.sem("l2f_ul")
    for j in range(NT):
        inst = q.dma_start(out=u_tm[:, j * D:(j + 1) * D], in_=usr[j])
        sul.tag(inst, 16)
    sul.wait(v)
    sul.wait(s)
    c.bufs["x_tm"] = u_tm
    emit_ln_entry(c, xn, rms_norm=False, stat_pref="e2")
    c.pv.wait(g)
    c.pt.wait(g)
    ps_tr2_ctx.__exit__(None, None, None)
    del c.bufs["ps_tr"]
    ps_moe = c.psum("l2f_psm", [128, 4096], F32)
    ps_g = ps_moe[:, 0:512]
    ps_v2 = ps_moe[:, 512:1024]
    c.pv.wait(t_)
    for a in range(ND):
        inst = v.tensor_scalar(out=cmc(xn, a), in0=cmc(xn, a),
                               scalar1=lnw[:, a * 4 + 2:a * 4 + 3],
                               scalar2=lnw[:, a * 4 + 3:a * 4 + 4],
                               op0=ALU.mult, op1=ALU.add)
    h2v = c.pv.tag(inst)
    c.pv.wait(t_, h2v)
    rw = c.sbuf("l2f_rw", [128, 8 * 4], F32)
    inst = g.dma_start(out=rw[:], in_=c.aps["l2_rw"])
    sw.tag(inst, 16)
    sw.wait(t_)
    rwb = c.sbuf("l2f_rwb", [128, 8 * 4], BF16)
    c.pg.wait(v)
    inst = v.tensor_copy(out=rwb[:], in_=rw[:])
    rv_ = c.pv.tag(inst)
    c.pv.wait(t_, rv_)
    for j in range(NT):
        for a in range(ND):
            inst = t_.matmul(ps_g[:, 0:4], cmc(xn, a, j * 128, (j + 1) * 128),
                             rwb[:, a * 4:(a + 1) * 4],
                             start=(a == 0), stop=(a == ND - 1))
        tv = c.pt.tag(inst)
        c.pt.wait(s, tv)
        inst = s.copy(out=logit[:, j * 4:(j + 1) * 4], in_=ps_g[:, 0:4])
        sv_ = c.ps_.tag(inst)
        c.ps_.wait(t_, sv_)
    c.ps_.wait(v)
    lg3 = logit[:].rearrange("p (j e) -> p j e", e=4)
    v.tensor_reduce(out=mtmp[:], in_=lg3, axis=AX, op=ALU.max)
    lcont = c.sbuf("l2f_lc", [128, 64], F32)
    for e in range(4):
        inst = v.tensor_copy(out=lcont[:, e * 16:(e + 1) * 16], in_=logit[:, e::4])
    c.pv.tag(inst)
    c.pv.wait(g)
    acc = c.sbuf("l2f_acc", [128, 16], F32)
    g.memset(acc[:], 1.0)
    for e in range(4):
        me = masks[:, e * 16:(e + 1) * 16]
        g.tensor_sub(out=me, in0=lcont[:, e * 16:(e + 1) * 16], in1=mtmp[:])
        g.tensor_scalar(out=me, in0=me, scalar1=0.0, scalar2=None, op0=ALU.is_ge)
        inst = g.tensor_mul(out=me, in0=me, in1=acc[:])
        if e < 3:
            g.tensor_scalar(out=c.bufs["l2f_mt2"][:], in0=me, scalar1=-1.0, scalar2=1.0,
                            op0=ALU.mult, op1=ALU.add)
            inst = g.tensor_mul(out=acc[:], in0=acc[:], in1=c.bufs["l2f_mt2"][:])
    c.pg.tag(inst)
    c.pg.wait(v)
    c.pg.wait(s)

    TGm = 1024
    wvg = c.aps["l2_wv_g"]
    wvv = c.aps["l2_wv_v"]
    prr = c.aps["l2_prj_s"]
    ps_ex = ps_moe
    lts2 = {}
    nl2 = [0]

    def loadw2(src, lo=0, hi=1024):
        par = nl2[0] % 2
        if nl2[0] >= 2:
            c.pt.wait(g, lts2.get(nl2[0] - 2, c.pt.n))
        inst = g.dma_start(out=wbuf[:, par * D: par * D + (hi - lo)], in_=src)
        swl.tag(inst, 16)
        idx = nl2[0]
        nl2[0] += 1
        return par, idx

    for tg in range(T // TGm):
        for e in range(4):
            for f in range(32):
                parg, idxg = loadw2(wvg[e][f])
                parv, idxv = loadw2(wvv[e][f])
                t_.wait_ge(swl.h, swl.n)
                for n2 in range(2):
                    lo = tg * TGm + n2 * 512
                    for a in range(ND):
                        inst = t_.matmul(ps_g[:], wbuf[:, parg * D + a * 128: parg * D + (a + 1) * 128],
                                         cmc(xn, a, lo, lo + 512),
                                         start=(a == 0), stop=(a == ND - 1))
                    gvv = c.pt.tag(inst)
                    lts2[idxg] = gvv
                    for a in range(ND):
                        inst = t_.matmul(ps_v2[:], wbuf[:, parv * D + a * 128: parv * D + (a + 1) * 128],
                                         cmc(xn, a, lo, lo + 512),
                                         start=(a == 0), stop=(a == ND - 1))
                    vvv = c.pt.tag(inst)
                    lts2[idxv] = vvv
                    c.pt.wait(s, gvv)
                    inst = s.activation(out=sgl[:], in_=ps_g[:], func=AF.Silu)
                    sv_ = c.ps_.tag(inst)
                    c.pt.wait(v, vvv)
                    c.ps_.wait(v, sv_)
                    inst = v.tensor_tensor(out=scr[:, f * TGm + n2 * 512: f * TGm + (n2 + 1) * 512],
                                           in0=ps_v2[:], in1=sgl[:], op=ALU.mult)
                    av = c.pv.tag(inst)
                    c.pv.wait(t_, av)
            for jjg in range(2):
                for f in range(32):
                    parp, idxp = loadw2(prr[e][f])
                    t_.wait_ge(swl.h, swl.n)
                    for jj in range(4):
                        jloc = jjg * 4 + jj
                        for n2 in range(2):
                            inst = t_.matmul(ps_ex[:, jj * 1024 + n2 * 512: jj * 1024 + (n2 + 1) * 512],
                                             scr[:, f * TGm + jloc * 128: f * TGm + (jloc + 1) * 128],
                                             wbuf[:, parp * D + n2 * 512: parp * D + (n2 + 1) * 512],
                                             start=(f == 0), stop=(f == 31))
                    lts2[idxp] = c.pt.tag(inst)
                tv = lts2[idxp]
                c.pt.wait(v, tv)
                for jj in range(4):
                    j = tg * 8 + jjg * 4 + jj
                    inst = v.scalar_tensor_tensor(out=u_tm[:, j * D:(j + 1) * D],
                                                  in0=ps_ex[:, jj * 1024:(jj + 1) * 1024],
                                                  scalar=acc_mask(c, masks, e, j),
                                                  in1=u_tm[:, j * D:(j + 1) * D],
                                                  op0=ALU.mult, op1=ALU.add)
                    av = c.pv.tag(inst)
                c.pv.wait(t_)
        for jj in range(8):
            j = tg * 8 + jj
            ex.feed(j, u_tm[:, j * D:(j + 1) * D])


def acc_mask(c, masks, e, j):
    return masks[:, e * 16 + j: e * 16 + j + 1]


def emit_l2_attn(c):
    nc, v, s, t_, g, q = c.nc, c.v, c.s, c.t, c.g, c.q
    ex = c.exit_ln
    xn = c.bufs["xn_cm"]
    scr = c.bufs["scr_bf"]
    Q0, K0, V0 = 0, NT * D, 2 * NT * D

    wT = c.sbuf("l2a_w", [128, 8 * D], BF16)
    rkb = c.sbuf("l2a_rkb", [128, 8 * D], BF16)
    rrow = c.sbuf("l2a_rrow", [1, D], BF16)
    onesb = c.sbuf("l2a_1b", [1, 128], BF16)
    outb = c.sbuf("l2a_ob", [1, D], BF16)
    kp = c.sbuf("l2a_kp", [128, D], BF16)
    prod = c.sbuf("l2a_prod", [128, D], F32)
    sc_all = c.sbuf("l2a_sc", [128, T], F32)
    p_all = c.sbuf("l2a_p", [128, T], BF16)
    mxb = c.sbuf("l2a_mx", [128, 128], F32)
    mx2 = c.sbuf("l2a_mx2", [128, 128], F32)
    dens = c.sbuf("l2a_dn", [128, 128], F32)
    lnd = c.sbuf("l2a_ln", [128, 128], F32)
    rdens = c.sbuf("l2a_rdn", [128, 128], F32)
    rdens2 = c.sbuf("l2a_rdn2", [128, 128], F32)
    acc = c.sbuf("l2a_acc", [128, D], F32)
    abf = kp
    acm = sc_all[:, 0:512].bitcast(BF16)
    ps_b = c.psum("l2a_psb", [128, D], F32)
    ps_o = c.psum("l2a_pso", [128, D], F32)
    sw = c.sem("l2a_w_s")

    wq = c.aps["l2a_wqT"].rearrange("(a p) e -> a p e", p=128)
    wk = c.aps["l2a_wkT"].rearrange("(a p) e -> a p e", p=128)
    wv = c.aps["l2a_wvT"].rearrange("(a p) e -> a p e", p=128)
    wo = c.aps["l2a_woT"].rearrange("(a p) e -> a p e", p=128)
    inst = g.dma_start(out=onesb[:], in_=c.aps["ones1"])
    sw.tag(inst, 16)
    inst = g.dma_start(out=outb[:], in_=c.aps["l2a_bo"])
    sw.tag(inst, 16)

    for wap, base in ((wq, Q0), (wk, K0), (wv, V0)):
        c.pt.wait(g)
        for a in range(ND):
            inst = g.dma_start(out=wT[:, a * D:(a + 1) * D], in_=wap[a])
            sw.tag(inst, 16)
        sw.wait(t_)
        for j in range(NT):
            for n2 in range(2):
                for a in range(ND):
                    inst = t_.matmul(ps_o[:, n2 * 512:(n2 + 1) * 512],
                                     cmc(xn, a, j * 128, (j + 1) * 128),
                                     wT[:, a * D + n2 * 512: a * D + (n2 + 1) * 512],
                                     start=(a == 0), stop=(a == ND - 1))
            tv = c.pt.tag(inst)
            c.pt.wait(v, tv)
            inst = v.tensor_copy(out=scr[:, base + j * D: base + (j + 1) * D], in_=ps_o[:])
            vv_ = c.pv.tag(inst)
            c.pv.wait(t_, vv_)
    c.pt.wait(g)
    for a in range(ND):
        inst = g.dma_start(out=wT[:, a * D:(a + 1) * D], in_=wo[a])
        sw.tag(inst, 16)
    sw.wait(t_)

    rk_rep = c.aps["l2a_rk_rep"]
    rv_rep = c.aps["l2a_rv_rep"]

    def bcast_rel(rep, s_, t2):
        c.pt.wait(g)
        inst = g.dma_start(out=rrow[:], in_=rep[s_ * 8 + t2: s_ * 8 + t2 + 1])
        sw.tag(inst, 16)
        sw.wait(t_)
        for n2 in range(2):
            inst = t_.matmul(ps_b[:, n2 * 512:(n2 + 1) * 512], onesb[:],
                             rrow[:, n2 * 512:(n2 + 1) * 512], start=True, stop=True)
        tv = c.pt.tag(inst)
        c.pt.wait(v, tv)
        inst = v.tensor_copy(out=rkb[:, t2 * D:(t2 + 1) * D], in_=ps_b[:])
        vv_ = c.pv.tag(inst)
        c.pv.wait(t_, vv_)

    for j in range(NT):
        s_ = j // 2
        if j % 2 == 0:
            for t2 in range(8):
                bcast_rel(rk_rep, s_, t2)
        for t2 in range(8):
            ktile = t2 * 2 + (j % 2)
            v.tensor_tensor(out=kp[:], in0=scr[:, K0 + ktile * D: K0 + (ktile + 1) * D],
                            in1=rkb[:, t2 * D:(t2 + 1) * D], op=ALU.add)
            v.tensor_tensor(out=prod[:], in0=scr[:, Q0 + j * D: Q0 + (j + 1) * D],
                            in1=kp[:], op=ALU.mult)
            inst = v.tensor_reduce(out=sc_all[:, j * 128 + t2 * 16: j * 128 + (t2 + 1) * 16],
                                   in_=prod[:].rearrange("p (h dd) -> p h dd", dd=64),
                                   axis=AX, op=ALU.add)
            c.pv.tag(inst)

    v.tensor_scalar(out=sc_all[:], in0=sc_all[:], scalar1=0.125,
                    scalar2=None, op0=ALU.mult)
    inst = v.tensor_reduce(out=mxb[:], in_=sc_all[:].rearrange("p (jt h) -> p jt h", h=16),
                           axis=AX, op=ALU.max)
    c.pv.tag(inst)
    c.pv.wait(g)
    inst = g.tensor_scalar(out=mx2[:], in0=mxb[:], scalar1=-1.0, scalar2=None, op0=ALU.mult)
    c.pg.tag(inst)
    c.pg.wait(s)
    for col in range(128):
        inst = s.activation(out=p_all[:, col * 16:(col + 1) * 16],
                            in_=sc_all[:, col * 16:(col + 1) * 16], func=AF.Exp,
                            bias=mx2[:, col:col + 1], accum_out=dens[:, col:col + 1])
    s.activation(out=lnd[:], in_=dens[:], func=AF.Ln)
    inst = s.activation(out=rdens[:], in_=lnd[:], func=AF.Exp, scale=-1.0)
    sv_ = c.ps_.tag(inst)
    c.ps_.wait(g, sv_)
    inst = g.tensor_copy(out=rdens2[:], in_=rdens[:])
    gv_ = c.pg.tag(inst)
    c.pg.wait(v, gv_)
    for col in range(128):
        v.tensor_scalar(out=p_all[:, col * 16:(col + 1) * 16],
                        in0=p_all[:, col * 16:(col + 1) * 16],
                        scalar1=rdens2[:, col:col + 1], scalar2=None, op0=ALU.mult)

    for j in range(NT):
        s_ = j // 2
        if j % 2 == 0:
            for t2 in range(8):
                bcast_rel(rv_rep, s_, t2)
        for t2 in range(8):
            ktile = t2 * 2 + (j % 2)
            v.tensor_tensor(out=kp[:], in0=scr[:, V0 + ktile * D: V0 + (ktile + 1) * D],
                            in1=rkb[:, t2 * D:(t2 + 1) * D], op=ALU.add)
            pb = p_all[:, j * 128 + t2 * 16: j * 128 + (t2 + 1) * 16]
            pb3 = pb.rearrange("p (h one) -> p h one", one=1).broadcast_to([128, 16, 64])
            if t2 == 0:
                v.tensor_tensor(out=acc[:].rearrange("p (h dd) -> p h dd", dd=64),
                                in0=kp[:].rearrange("p (h dd) -> p h dd", dd=64),
                                in1=pb3, op=ALU.mult)
            else:
                v.tensor_tensor(out=prod[:].rearrange("p (h dd) -> p h dd", dd=64),
                                in0=kp[:].rearrange("p (h dd) -> p h dd", dd=64),
                                in1=pb3, op=ALU.mult)
                v.tensor_tensor(out=acc[:], in0=acc[:], in1=prod[:], op=ALU.add)
        inst = v.tensor_copy(out=abf[:], in_=acc[:])
        av = c.pv.tag(inst)
        c.ps_.wait(t_)
        c.pv.wait(t_, av)
        pst = c.bufs["ps_tr"]
        for grp in range(2):
            c.pv.wait(t_)
            for k in range(4):
                ch = grp * 4 + k
                inst = t_.transpose(pst[:, k * 1024: k * 1024 + 128],
                                    abf[:, ch * 128:(ch + 1) * 128],
                                    c.bufs["ident_bf"][:])
            tv = c.pt.tag(inst)
            c.pt.wait(v, tv)
            for k in range(4):
                ch = grp * 4 + k
                inst = v.tensor_copy(out=acm[:, ch * 128:(ch + 1) * 128],
                                     in_=pst[:, k * 1024: k * 1024 + 128])
            c.pv.tag(inst)
        c.pv.wait(t_)
        for n2 in range(2):
            for a in range(ND):
                t_.matmul(ps_o[:, n2 * 512:(n2 + 1) * 512],
                          acm[:, a * 128:(a + 1) * 128],
                          wT[:, a * D + n2 * 512: a * D + (n2 + 1) * 512],
                          start=(a == 0), stop=False)
            inst = t_.matmul(ps_o[:, n2 * 512:(n2 + 1) * 512], onesb[:],
                             outb[:, n2 * 512:(n2 + 1) * 512], start=False, stop=True)
        tv = c.pt.tag(inst)
        c.pt.wait(v, tv)
        ex.feed(j, ps_o[:])
        c.pv.wait(t_)


BLOCK_EMIT = {"l3_ffn": emit_l3_ffn, "l1_ffn": emit_l1_ffn,
              "l0_ffn": emit_l0_ffn, "l3_attn": emit_l3_attn,
              "l0_attn": emit_l0_attn, "l1_attn": emit_l1_attn,
              "l2_ffn": emit_l2_ffn, "l2_attn": emit_l2_attn}



def dbg_dump(c, name, ap):
    if name not in getattr(c, "dbg_out", {}):
        return
    sd = c.sem("dbgd")
    c.pv.wait(c.g)
    c.ps_.wait(c.g)
    inst = c.g.dma_start(out=c.dbg_out[name], in_=ap)
    sd.tag(inst, 16)
    sd.wait(c.g)


X_STREAM_BLOCKS = {"l1_ffn", "l3_attn", "l0_attn", "l1_attn", "l2_ffn",
                   "l2_attn"}
SCR_COLS = {"l1_ffn": 8192, "l3_ffn": 12288, "l0_ffn": 512, "l3_attn": 512,
            "l0_attn": 8 * T, "l1_attn": 59392, "l2_ffn": 32768,
            "l2_attn": 3 * NT * D}


def emit_whole_block(c, block, x_in, y_out):
    nc, v, s, t_, g, q = c.nc, c.v, c.s, c.t, c.g, c.q
    if c.bar is not None and c.bar.n > 0:
        for eng in (q, g, v, s, t_):
            c.bar.wait(eng)
    x_stream = block in X_STREAM_BLOCKS
    if x_stream:
        _, xbt_ctx = c.sbuf_scoped("xbt_s", [128, 2 * D], BF16)
        _, x_cm_ctx = c.sbuf_scoped("x_tm", [128, NT * D], F32)
    else:
        c.sbuf("x_tm", [128, NT * D], F32)
        c.sbuf("xbt_s", [128, 2 * D], BF16)
    xn_cm = c.sbuf("xn_cm", [128, ND * CW], BF16)
    if not x_stream:
        c.sbuf("scr_bf", [128, SCR_COLS.get(block, 16384)], BF16)
    ident = c.sbuf("ident_bf", [128, 128], BF16)
    c.sbuf("onescol_f", [128, 1], F32)
    c.sbuf("onesrow_f", [1, 128], F32)
    _, ps_tr_ctx = c.psum_scoped("ps_tr", [128, 4096], BF16)
    grp = 2 if block in ("l1_ffn", "l1_attn", "l2_ffn", "l2_attn") else 4
    c.exit_ln = ExitLN(c, x_stream=x_stream, grp=grp)
    if x_stream:
        c.exit_ln.x_src = x_in.rearrange("(n p) d -> n p d", p=128)
        c.exit_ln.x_dst = y_out.rearrange("(n p) d -> n p d", p=128)

    sldg = c.sem("ld_g")
    sldx = c.sem("ld_x")
    inst = g.dma_start(out=ident[:], in_=c.aps["ident"])
    sldg.tag(inst, 16)
    xr = x_in.rearrange("(n p) d -> n p d", p=128)
    for j in range(NT):
        inst = q.dma_start(out=xt(c, j), in_=xr[j])
        sldx.tag(inst, 16)
    c.reg_consts([LN_EPS, 1e-6])
    for ch in range(ND):
        inst = v.memset(xn_cm[:, ch * CW: ch * CW + PAD], 0.0)
    v.memset(c.bufs["onescol_f"][:], 1.0)
    inst = v.memset(c.bufs["onesrow_f"][:], 1.0)
    c.pv.tag(inst)
    sldx.wait(v)
    sldx.wait(s)
    sldx.wait(t_)
    sldg.wait(t_)
    c.pg.wait(s)

    rms = block in ("l0_attn", "l2_ffn")
    emit_ln_entry(c, xn_cm, rms_norm=rms,
                  rms_eps=(LN_EPS if block == "l2_ffn" else 1e-6))
    dbg_dump(c, "dbg_xn", xn_cm[:, 0:2 * CW])
    c.pv.wait(g)
    c.ps_.wait(g)
    c.pt.wait(g)
    if block != "l2_attn":
        ps_tr_ctx.__exit__(None, None, None)
        del c.bufs["ps_tr"]
    c.pv.wait(t_)
    if x_stream:
        x_cm_ctx.__exit__(None, None, None)
        del c.bufs["x_tm"]
        if block != "l2_ffn":
            xbt_ctx.__exit__(None, None, None)
            del c.bufs["xbt_s"]
        c.sbuf("scr_bf", [128, SCR_COLS.get(block, 16384)], BF16)

    BLOCK_EMIT[block](c)

    if x_stream:
        ex = c.exit_ln
        ex.sstx.wait(q)
    else:
        sst = c.sem("st_x")
        c.pv.wait(q)
        c.ps_.wait(q)
        yr = y_out.rearrange("(n p) d -> n p d", p=128)
        for j in range(NT):
            inst = q.dma_start(out=yr[j], in_=xt(c, j))
            sst.tag(inst, 16)
        sst.wait(q)
    if c.bar is not None and not c.is_last:
        for nm, sm in list(c.sems.items()):
            if sm is c.bar or sm.n == 0:
                continue
            sm.wait(v)
            sm.wait(s)
            sm.wait(g)
        bb = c.bar_buf
        c.bar.tag(v.memset(bb[:, 0:1], 0.0))
        c.bar.tag(g.memset(bb[:, 1:2], 0.0))
        c.bar.tag(s.copy(out=bb[:, 2:3], in_=bb[:, 3:4]))


DBG_DUMPS = []


def build_block_program(block, meta):
    import contextlib
    nc = bass.Bass("TRN2", target_bir_lowering=False,
                   detect_race_conditions=False)
    stack = contextlib.ExitStack()
    c = Ctx(nc, stack)
    c.dbg_out = {}
    for nm, shp in DBG_DUMPS:
        c.dbg_out[nm] = nc.dram_tensor(nm, list(shp), F32, kind="ExternalOutput").ap()
    x_in = nc.dram_tensor("x", [T, D], F32, kind="ExternalInput").ap()
    y_out = nc.dram_tensor("y", [T, D], F32, kind="ExternalOutput").ap()
    for name, arr in meta.items():
        dt = {np.dtype(np.float32): F32, np.dtype(np.int32): I32}[arr.dtype]
        c.aps[name] = nc.dram_tensor(name, list(arr.shape), dt, kind="ExternalInput").ap()
    with stack:
        with nc.Block() as block_:
            @block_.sync
            def _(_):
                emit_whole_block(c, block, x_in, y_out)
    return nc


def build_fused_program(blocks, metas):
    import contextlib
    nc = bass.Bass("TRN2", target_bir_lowering=False,
                   detect_race_conditions=False)
    outer = contextlib.ExitStack()
    sems = {}
    x_in = nc.dram_tensor("x", [T, D], F32, kind="ExternalInput").ap()
    y_out = nc.dram_tensor("y", [T, D], F32, kind="ExternalOutput").ap()
    hs = [nc.dram_tensor(f"h{k}", [T, D], F32, kind="Internal").ap()
          for k in range(len(blocks) - 1)]
    ctxs = []
    for k in range(len(blocks)):
        c = Ctx(nc, None, prefix=f"b{k}_", sems=sems, sem_stack=outer,
                const_stack=outer)
        c.dbg_out = {}
        for name, arr in metas[k].items():
            dt = {np.dtype(np.float32): F32, np.dtype(np.int32): I32}[arr.dtype]
            c.aps[name] = nc.dram_tensor(f"b{k}_" + name, list(arr.shape), dt,
                                         kind="ExternalInput").ap()
        ctxs.append(c)
    with outer:
        with nc.Block() as block_:
            @block_.sync
            def _(_):
                barb = outer.enter_context(
                    nc.sbuf_tensor("barb", [128, 4], F32))
                boot = Ctx(nc, outer, prefix="cc_", sems=sems,
                           sem_stack=outer, const_stack=outer)
                boot.reg_consts([LN_EPS, 1e-6])
                boot.pg.tag(nc.gpsimd.memset(barb[:], 0.0))
                bar = boot.sem("bar")
                for k, blk in enumerate(blocks):
                    c = ctxs[k]
                    c.bar = bar
                    c.bar_buf = barb
                    c.is_last = (k == len(blocks) - 1)
                    c.stack = contextlib.ExitStack()
                    xi = x_in if k == 0 else hs[k - 1]
                    yo = y_out if c.is_last else hs[k]
                    with c.stack:
                        emit_whole_block(c, blk, xi, yo)
    return nc



def prep_common():
    return {"ident": np.eye(128, dtype=np.float32),
            "ones1": np.ones((1, 128), dtype=np.float32)}


def lhs_slabs(wT, kb=128, mb=128):
    K, M = wT.shape
    nk, nm = K // kb, M // mb
    return np.ascontiguousarray(
        wT.reshape(nk, kb, nm, mb).transpose(2, 1, 0, 3).reshape(nm, kb, nk * mb))


def prep_l3_ffn(inputs):
    d = prep_common()
    wgT = inputs["L3_ffn_w_gate"].T.astype(np.int32)
    wuT = inputs["L3_ffn_w_up"].T.astype(np.int32)
    d["l3_wg_s"] = lhs_slabs(wgT)
    d["l3_wu_s"] = lhs_slabs(wuT)
    d["l3_wdT"] = np.ascontiguousarray(inputs["L3_ffn_w_down"].T).astype(np.int32)
    return d


def prep_l1_ffn(inputs):
    d = prep_common()
    w1T = np.ascontiguousarray(inputs["L1_ffn_W1"].T).astype(np.float32)
    d["l1_w1_s"] = lhs_slabs(w1T)
    d["l1_w2T"] = np.ascontiguousarray(inputs["L1_ffn_W2"].T).astype(np.float32)
    d["l1_b1"] = np.ascontiguousarray(inputs["L1_ffn_b1"].reshape(32, 128).T).astype(np.float32)
    d["l1_b2"] = inputs["L1_ffn_b2"].reshape(1, 1024).astype(np.float32)
    return d


def prep_l0_ffn(inputs):
    d = prep_common()
    d["l0_wrT"] = np.ascontiguousarray(inputs["L0_ffn_W_r"].T).astype(np.float32)
    d["l0_wk_s"] = lhs_slabs(np.ascontiguousarray(inputs["L0_ffn_W_k"].T).astype(np.float32))
    d["l0_wvT"] = np.ascontiguousarray(inputs["L0_ffn_W_v"].T).astype(np.float32)
    d["l0_mur"] = np.ascontiguousarray(inputs["L0_ffn_mu_r"].reshape(8, 128).T).astype(np.float32)
    d["l0_muk"] = np.ascontiguousarray(inputs["L0_ffn_mu_k"].reshape(8, 128).T).astype(np.float32)
    return d


def prep_l3_attn(inputs):
    d = prep_common()
    d["l3a_iwT"] = np.ascontiguousarray(inputs["L3_attn_input_weights"].T).astype(np.float32)
    d["l3a_owT"] = np.ascontiguousarray(inputs["L3_attn_output_weight"].T).astype(np.float32)
    STD = np.array([1., 2., 4., 8., 1., 2., 4., 8.])
    OFF = np.array([-1., -2., -4., -8., -1., -2., -4., -8.])
    deltas = np.array([-128, 0, 128, 256, 384, 512])
    p_ = np.arange(128)[None, None, :, None]
    qq = np.arange(512)[None, None, None, :]
    dist = deltas[None, :, None, None] + p_ - qq
    z = (dist - OFF[:, None, None, None]) / STD[:, None, None, None]
    pt = (1.0 / (STD[:, None, None, None] * np.sqrt(2 * np.pi))
          * np.exp(-0.5 * z * z))
    d["l3a_pT"] = np.ascontiguousarray(
        pt.transpose(0, 2, 1, 3).reshape(8, 128, 6 * 512)).astype(np.float32)
    return d


def prep_l0_attn(inputs):
    d = prep_common()
    d["l0_inw_s"] = lhs_slabs(inputs["L0_attn_in_w"].astype(np.float32))
    d["l0_pw_s"] = lhs_slabs(inputs["L0_attn_param_w"].astype(np.float32))
    d["l0_owT"] = np.ascontiguousarray(inputs["L0_attn_out_w"]).astype(np.float32)
    d["l0_nw"] = np.ascontiguousarray(inputs["L0_attn_norm_w"].reshape(8, 128).T).astype(np.float32)
    d["l0_inb"] = np.ascontiguousarray(inputs["L0_attn_in_b"].reshape(16, 128).T).astype(np.float32)
    d["l0_pb"] = np.ascontiguousarray(inputs["L0_attn_param_b"].reshape(24, 128).T).astype(np.float32)
    d["l0_cw"] = np.ascontiguousarray(
        inputs["L0_attn_conv_w"].T.reshape(8, 128, 4).transpose(1, 0, 2).reshape(128, 32)).astype(np.float32)
    d["l0_cb"] = np.ascontiguousarray(inputs["L0_attn_conv_b"].reshape(8, 128).T).astype(np.float32)
    d["l0_skip"] = np.ascontiguousarray(inputs["L0_attn_skip"].reshape(8, 128).T).astype(np.float32)
    d["l0_outb"] = inputs["L0_attn_out_b"].reshape(1, 1024).astype(np.float32)
    return d


def prep_l1_attn(inputs):
    d = prep_common()
    f32 = np.float32
    d["l1_wO"] = np.ascontiguousarray(inputs["L1_attn_W_O"].reshape(8, 128, 1024)).astype(f32)
    d["l1_uk_s"] = lhs_slabs(inputs["L1_attn_W_UK"].astype(f32))
    d["l1_wUV"] = np.ascontiguousarray(inputs["L1_attn_W_UV"].reshape(2, 128, 1024)).astype(f32)
    d["l1_dkv_s"] = lhs_slabs(inputs["L1_attn_W_DKV"].astype(f32))
    wkr = inputs["L1_attn_W_KR"].astype(f32)
    z = np.zeros((1024, 32), f32)
    d["l1_wKR128"] = np.ascontiguousarray(
        np.concatenate([wkr, z, wkr, z], axis=1).reshape(1024, 128).reshape(8, 128, 128))
    d["l1_dq_s"] = lhs_slabs(inputs["L1_attn_W_DQ"].astype(f32))
    d["l1_uq_s"] = lhs_slabs(inputs["L1_attn_W_UQ"].astype(f32))
    wqr = inputs["L1_attn_W_QR"].astype(f32)
    qr64 = np.zeros((4096, 1024), f32)
    for h in range(16):
        qr64[:, h * 64:h * 64 + 32] = wqr[:, h * 32:(h + 1) * 32]
    d["l1_qr_s"] = lhs_slabs(qr64)
    DHR = 32
    tt = np.arange(T, dtype=np.float64)
    inv = 1.0 / 10000.0 ** (np.arange(0, DHR, 2) / DHR)
    fr = tt[:, None] * inv[None, :]
    emb = np.concatenate([fr, fr], -1)
    sc32 = (np.cos(emb) + np.sin(emb)).astype(f32)
    rep = np.zeros((128, T), f32)
    for r in range(128):
        dd_ = r % 64
        if dd_ < 32:
            rep[r] = sc32[:, dd_]
    d["l1_scale128"] = rep
    tri = np.where(np.arange(128)[None, :] <= np.arange(128)[:, None], 0.0, 1e30)
    d["l1_trimask"] = tri.astype(f32)
    return d


def prep_l2_ffn(inputs):
    d = prep_common()
    f32 = np.float32
    aw = inputs["L2_ffn_attn_w"].astype(f32)
    d["l2_aw_s"] = lhs_slabs(aw)
    d["l2_aw_rows"] = np.ascontiguousarray(aw.reshape(8, 128, 3072))
    d["l2_projw_r"] = np.ascontiguousarray(inputs["L2_ffn_proj_w"].astype(f32).reshape(8, 128, 1024))
    lnw = np.stack([inputs["L2_ffn_ln1_w"], inputs["L2_ffn_ln1_b"],
                    inputs["L2_ffn_ln2_w"], inputs["L2_ffn_ln2_b"]], 1)
    d["l2_lnw"] = np.ascontiguousarray(lnw.reshape(8, 128, 4).transpose(1, 0, 2).reshape(128, 32)).astype(f32)
    d["l2_cw"] = np.ascontiguousarray(
        inputs["L2_ffn_conv_w"].reshape(1024, 3).reshape(8, 128, 3).transpose(1, 0, 2).reshape(128, 24)).astype(f32)
    Dh = 64
    R = np.zeros((128, 128), f32)
    for hh in range(2):
        for i in range(Dh):
            if i < 32:
                R[hh * 64 + i, hh * 64 + i + 32] = -1.0
            else:
                R[hh * 64 + i, hh * 64 + i - 32] = 1.0
    d["l2_rt"] = np.ascontiguousarray(R.T)
    if "l1_trimask" not in d:
        tri = np.where(np.arange(128)[None, :] <= np.arange(128)[:, None], 0.0, 1e30)
        d["l1_trimask"] = tri.astype(f32)
    invf = 1.0 / 10000.0 ** (np.arange(0, Dh, 2) / Dh)
    fr = np.arange(T)[:, None] * invf[None, :]
    emb = np.concatenate([fr, fr], -1)
    cos, sin = np.cos(emb).astype(f32), np.sin(emb).astype(f32)
    rep_c = np.zeros((128, T), f32)
    rep_s = np.zeros((128, T), f32)
    for r in range(128):
        rep_c[r] = cos[:, r % 64]
        rep_s[r] = sin[:, r % 64]
    d["l2_cosr"] = rep_c
    d["l2_sinr"] = rep_s
    d["l2_rw"] = np.ascontiguousarray(
        inputs["L2_ffn_router_w"].astype(f32).reshape(8, 128, 4).transpose(1, 0, 2).reshape(128, 32))
    wv = inputs["L2_ffn_expert_wv"].astype(f32)
    pj = inputs["L2_ffn_expert_proj"].astype(f32)
    d["l2_wv_g"] = np.stack([lhs_slabs(wv[e][:, :4096]) for e in range(4)])
    d["l2_wv_v"] = np.stack([lhs_slabs(wv[e][:, 4096:]) for e in range(4)])
    d["l2_prj_s"] = np.stack([np.ascontiguousarray(pj[e].reshape(32, 128, 1024)) for e in range(4)])
    return d


def prep_l2_attn(inputs):
    d = prep_common()
    f32 = np.float32
    d["l2a_wqT"] = np.ascontiguousarray(inputs["L2_attn_w_q"].T).astype(f32)
    d["l2a_wkT"] = np.ascontiguousarray(inputs["L2_attn_w_k"].T).astype(f32)
    d["l2a_wvT"] = np.ascontiguousarray(inputs["L2_attn_w_v"].T).astype(f32)
    d["l2a_woT"] = np.ascontiguousarray(inputs["L2_attn_w_o"].T).astype(f32)
    d["l2a_bo"] = inputs["L2_attn_b_o"].reshape(1, 1024).astype(f32)
    pos = np.arange(8)
    rel = np.clip(pos[:, None] - pos[None, :], -128, 128) + 128
    rk = inputs["L2_attn_rel_k"].astype(f32)[rel]
    rv = inputs["L2_attn_rel_v"].astype(f32)[rel]
    d["l2a_rk_rep"] = np.ascontiguousarray(np.tile(rk.reshape(64, 64), (1, 16)))
    d["l2a_rv_rep"] = np.ascontiguousarray(np.tile(rv.reshape(64, 64), (1, 16)))
    return d


PREP = {"l3_ffn": prep_l3_ffn, "l1_ffn": prep_l1_ffn, "l0_ffn": prep_l0_ffn,
        "l3_attn": prep_l3_attn, "l0_attn": prep_l0_attn, "l1_attn": prep_l1_attn,
        "l2_ffn": prep_l2_ffn, "l2_attn": prep_l2_attn}

BLOCK_ORDER = ["l0_attn", "l0_ffn", "l1_attn", "l1_ffn",
               "l2_attn", "l2_ffn", "l3_attn", "l3_ffn"]


def shard_x(block, x_full, core):
    if block == "l2_attn":
        return np.ascontiguousarray(
            x_full[:, core * 256:(core + 1) * 256, :].reshape(T, D))
    return np.ascontiguousarray(x_full[core])


def unshard_y(block, ys):
    if block == "l2_attn":
        out = np.empty((NCORES, T, D), np.float32)
        for cidx in range(NCORES):
            out[:, cidx * 256:(cidx + 1) * 256, :] = ys[cidx].reshape(8, 256, D)
        return out
    return np.stack(ys, axis=0)


def run_block(block, x_full, inputs):
    meta = PREP[block](inputs)
    nc = build_block_program(block, meta)
    in_maps = []
    for b in range(NCORES):
        m = {"x": shard_x(block, x_full, b)}
        m.update(meta)
        in_maps.append(m)
    res = run_bass_kernel_spmd(nc, in_maps, core_ids=list(range(NCORES)))
    return unshard_y(block, [res.results[b]["y"] for b in range(NCORES)])


def run_group(blocks, x_full, inputs):
    metas = [PREP[b](inputs) for b in blocks]
    nc = build_fused_program(blocks, metas)
    in_maps = []
    for b in range(NCORES):
        m = {"x": shard_x(blocks[0], x_full, b)}
        for k, meta in enumerate(metas):
            for nm, arr in meta.items():
                m[f"b{k}_{nm}"] = arr
        in_maps.append(m)
    res = run_bass_kernel_spmd(nc, in_maps, core_ids=list(range(NCORES)))
    return unshard_y(blocks[-1], [res.results[b]["y"] for b in range(NCORES)])


GROUPS = [["l0_attn", "l0_ffn", "l1_attn", "l1_ffn"],
          ["l2_attn"],
          ["l2_ffn", "l3_attn", "l3_ffn"]]


def kernel(**inputs):
    x = np.asarray(inputs["L0_attn_x"], dtype=np.float32)
    for grp in GROUPS:
        if len(grp) == 1:
            x = run_block(grp[0], x, inputs)
        else:
            x = run_group(grp, x, inputs)
    return x
